# version 25
# baseline (speedup 1.0000x reference)
import contextlib
import numpy as np
import concourse.bass as bass
import concourse.mybir as mybir
from concourse.bass_utils import run_bass_kernel_spmd

F32 = mybir.dt.float32
BF16 = mybir.dt.bfloat16
ALU = mybir.AluOpType
AF = mybir.ActivationFunctionType
AX = mybir.AxisListType.X

D = 2048
DT = 16
NB = 2
S = 2048
NTOK = NB * S
NTT = NTOK // 128
H = 16
C = 128
NCH = S // C
RANK = 64
NE = 8
RA = NE * RANK
EPS = 1e-5
FSCALE = 0.5 / RANK

ENGINES = ("sync", "tensor", "vector", "scalar", "gpsimd")

STAGES = ("P1",)


class Sched:

    def __init__(self, nc):
        self.nc = nc
        self.steps = {e: [] for e in ENGINES}
        self.sems = {}
        self.cnt = {}
        self._ctx = contextlib.ExitStack()
        for s in ["T", "V", "Sc", "G"]:
            self._mksem(s)

    def _mksem(self, s):
        self.sems[s] = self._ctx.enter_context(self.nc.semaphore(s))
        self.cnt[s] = 0

    def at(self, name):
        if name not in self.cnt:
            self._mksem(name)
        return (name, self.cnt[name])

    def step(self, eng, fn, waits=(), inc=None, inc_n=1):
        w = tuple((s, v) for t in waits if t is not None for (s, v) in (t,) if v > 0)
        self.steps[eng].append((w, fn, inc, inc_n))
        if inc is not None:
            self.cnt[inc] += inc_n
            return (inc, self.cnt[inc])
        return None

    def dma(self, qeng, cls, out, in_, waits=(), **kw):
        if cls not in self.cnt:
            self._mksem(cls)
        def fn(e, out=out, in_=in_, kw=kw):
            return e.dma_start(out, in_, **kw)
        return self.step(qeng, fn, waits=waits, inc=cls, inc_n=16)

    def dmat(self, qeng, cls, out, in_, waits=()):
        if cls not in self.cnt:
            self._mksem(cls)
        def fn(e, out=out, in_=in_):
            return e.dma_start_transpose(out, in_)
        return self.step(qeng, fn, waits=waits, inc=cls, inc_n=16)

    def run(self, eng_name, eng):
        for waits, fn, inc, inc_n in self.steps[eng_name]:
            for s, v in waits:
                eng.wait_ge(self.sems[s], v)
            ins = fn(eng)
            if inc is not None:
                assert ins is not None, f"{eng_name} step with inc={inc} returned None"
                ins.then_inc(self.sems[inc], inc_n)

    def emit_and_close(self):
        with self.nc.Block() as block:
            for e in ENGINES:
                getattr(block, e)(lambda eng, e=e: self.run(e, eng))
        self._ctx.close()


def build(stages=STAGES, debug=()):
    nc = bass.Bass("TRN2", target_bir_lowering=False)

    def dram(name, shape, dtype, kind="Internal"):
        k = "ExternalOutput" if name in debug else kind
        return nc.dram_tensor(name, shape, dtype, kind=k).ap()

    EI = lambda n, s, d: nc.dram_tensor(n, s, d, kind="ExternalInput").ap()
    xs = EI("xs", [NTOK, D], F32)
    wq, wk, wv, wg, wo = (EI(n, [D, D], F32) for n in ["wq", "wk", "wv", "wg", "wo"])
    bq, bk, bv, bg, bo = (EI(n, [D], F32) for n in ["bq", "bk", "bv", "bg", "bo"])
    gnw, gnb = EI("gnw", [D], F32), EI("gnb", [D], F32)
    gamma = EI("gamma", [H], F32)
    fAT = EI("fAT", [128, DT * RA], F32)
    fBT = EI("fBT", [128, 4 * D], F32)
    fWrT = EI("fWrT", [128, DT * NE], F32)
    fbr = EI("fbr", [1, NE], F32)
    identf = EI("identf", [128, 128], F32)
    identb = EI("identb", [128, 128], BF16)
    onesrow_b = EI("onesrow_b", [1, 128], BF16)
    onesrow_f = EI("onesrow_f", [1, 128], F32)
    ones128f = EI("ones128f", [128, 128], F32)
    onescol_f = EI("onescol_f", [128, 1], F32)
    expnT = EI("expnT", [128, 128], F32)
    tpow = EI("tpow", [1, D], F32)
    HREP = EI("HREP", [H, D], F32)
    cvals = EI("cvals", [128, 8], F32)
    Esel = EI("Esel", [NE, RA], BF16)

    y = dram("y", [NTOK, D], F32, kind="ExternalOutput")
    wqB = dram("wqB", [D, D], BF16)
    wkB = dram("wkB", [D, D], BF16)
    wvB = dram("wvB", [D, D], BF16)
    wgB = dram("wgB", [D, D], BF16)
    woB = dram("woB", [D, D], BF16)
    x1t = dram("x1t", [NTOK, D], BF16)
    x1f = dram("x1f", [NTOK, D], F32)
    x2t = dram("x2t", [NTOK, D], BF16)
    x3f = dram("x3f", [NTOK, D], F32)
    x4t = dram("x4t", [NTOK, D], BF16)
    QT = dram("QT", [NB, NCH, DT, 128, C], BF16)
    KT = dram("KT", [NB, NCH, DT, 128, C], BF16)
    GT = dram("GT", [NB, NCH, DT, 128, C], BF16)
    ROT = dram("ROT", [NB, NCH, DT, 128, C], BF16)
    VT = dram("VT", [NB, NCH, DT, 128, C], BF16)

    sc = Sched(nc)
    A = nc.alloc_sbuf_tensor

    def _trmm(t, src_ap):
        last = None
        for h in range(H):
            last = t.matmul(ps_h[:, h * 128:(h + 1) * 128],
                            src_ap[:, h * 128:(h + 1) * 128], identb_sb[:],
                            start=(h % 4 == 0), stop=(h % 4 == 3))
        return last


    abuf = A("abuf", [128, DT * RA], BF16)
    bbuf = A("bbuf", [128, 4 * D], BF16)
    wrbuf = A("wrbuf", [128, DT * NE], BF16)
    dmask = A("dmask", [128, H * C], BF16)
    abc = A("abc", [128, D], BF16)
    bbcT = A("bbcT", [128, D], BF16)
    identb_sb = A("identb_sb", [128, 128], BF16)
    identf_sb = A("identf_sb", [128, 128], F32)
    onesrow_b_sb = A("onesrow_b_sb", [1, 128], BF16)
    onesrow_f_sb = A("onesrow_f_sb", [1, 128], F32)
    onescol_f_sb = A("onescol_f_sb", [128, 1], F32)
    esel_sb = A("esel_sb", [NE, RA], BF16)
    biases = A("biases", [128, 6 * DT], F32)
    biasrow = A("biasrow", [1, D], BF16)
    brrow = A("brrow", [1, NE], BF16)
    gamrow = A("gamrow", [1, H], F32)
    lngrow = A("lngrow", [1, H], F32)
    lngcol16 = A("lngcol16", [H, 1], F32)
    lncol = A("lncol", [128, H], F32)
    gccol = A("gccol", [128, H], F32)
    gcbc = A("gcbc", [128, D], F32)
    stats = A("stats", [128, 256], F32)
    cvals_sb = A("cvals_sb", [128, 8], F32)
    CVALS = [EPS, -0.5, float(C), FSCALE, 1.0 / D, 2.0]
    for ci, cv in enumerate(CVALS):
        nc.const_aps.register(F32, cv, cvals_sb[:, ci:ci + 1])

    GL = 1024
    GLH = 512
    NG = NTOK // GL
    xTg = A("xTg", [128, DT * GL], BF16)
    dtmp = A("dtmp", [128, 128], F32)
    expnT_sb = A("expnT_sb", [128, 128], F32)
    ones128f_sb = A("ones128f_sb", [128, 128], F32)
    WSTG = [A(f"WSTG{i}", [128, 2 * D], BF16) for i in range(2)]
    wtile = [A(f"wtile{i}", [128, DT * 128], BF16) for i in range(2)]
    qkstage = [A(f"qkst{i}", [128, 1024], BF16) for i in range(2)]
    wtile3 = [wtile[0][:], wtile[1][:]]
    xbuf = [A(f"xbuf{i}", [128, D], F32) for i in range(2)]
    sjunk = A("sjunk", [128, D], BF16)
    xtout = [A(f"xtout{i}", [128, D], BF16) for i in range(2)]
    hbuf = [A(f"hbuf{i}", [128, 4 * 512], BF16) for i in range(2)]
    mexp = [A(f"mexp{i}", [128, 4 * 512], BF16) for i in range(2)]
    mwT = [A(f"mwT{i}", [NE, 512], BF16) for i in range(2)]
    mwtile = [A(f"mwtile{i}", [128, NE], F32) for i in range(4)]
    etmp = [A(f"etmp{i}", [128, NE], F32) for i in range(4)]
    mtmp = [A(f"mtmp{i}", [128, NE], F32) for i in range(4)]
    xres = [A(f"xres{i}", [128, D], F32) for i in range(2)]
    oub = [A(f"oub{i}", [128, D], F32) for i in range(2)]
    hrep_sb = oub[0][0:16, 0:D]
    lngexp = oub[1][0:1, 0:D]
    r1row = xres[0][0:1, 0:D]
    tpow_sb = xres[1][0:1, 0:D]

    P = nc.alloc_psum_tensor
    ps_h = P("ps_h", [128, 4 * 512], F32)
    ps_r45 = P("ps_r45", [128, 1024], F32)
    ps_sel = P("ps_sel", [128, 1024], F32)

    for dst, src in [(identb_sb[:], identb), (identf_sb[:], identf),
                     (onesrow_b_sb[:], onesrow_b), (onesrow_f_sb[:], onesrow_f),
                     (onescol_f_sb[:], onescol_f), (ones128f_sb[:], ones128f),
                     (expnT_sb[:], expnT), (esel_sb[:], Esel),
                     (gamrow[:], gamma.rearrange("(o h) -> o h", o=1)),
                     (cvals_sb[:], cvals),
                     (hrep_sb, HREP), (tpow_sb, tpow)]:
        sc.dma("sync", "c0", dst, src)
    for i, b in enumerate([bq, bk, bg, gnw, gnb, bv]):
        sc.dma("sync", "c0", biases[:, i * DT:(i + 1) * DT],
               b.rearrange("(t p) -> p t", p=128), allow_slow_non_contiguous=True)
    sc.dma("sync", "c0", lngcol16[:], gamma.rearrange("(h o) -> h o", o=1),
           allow_slow_non_contiguous=True)
    cl = sc.at("c0")

    borow_tok = sc.dma("gpsimd", "bo0", biasrow[:], bo.rearrange("(o d) -> o d", o=1))
    brrow_tok = sc.dma("gpsimd", "br0", brrow[:], fbr)
    abT = sc.dma("gpsimd", "prep", abuf[:], fAT)
    bbT = sc.dma("gpsimd", "prep", bbuf[:], fBT, waits=[abT])
    prep_tok = sc.dma("gpsimd", "prep", wrbuf[:], fWrT, waits=[bbT])

    r2f = lambda ap, p=128: ap.rearrange("(a p) d -> p a d", p=p)
    stg3 = [WSTG[i].rearrange("p (a d) -> p a d", d=D) for i in range(2)]
    prev_st = [None, None]
    wprep = {}
    for w, wB, nm in [(wq, wqB, "q"), (wk, wkB, "k"), (wv, wvB, "v"),
                      (wg, wgB, "g"), (wo, woB, "o")]:
        wr3 = r2f(w)
        wB3 = r2f(wB)
        for qtr in range(8):
            pp = qtr % 2
            ld = sc.dma("gpsimd", f"wc{pp}", stg3[pp][:, 0:2, :],
                        wr3[:, qtr * 2:(qtr + 1) * 2, :],
                        waits=[prev_st[pp]] if prev_st[pp] else [])
            prev_st[pp] = sc.dma("gpsimd", f"wc{pp}", wB3[:, qtr * 2:(qtr + 1) * 2, :],
                                 stg3[pp][:, 0:2, :], waits=[ld])
        wprep[nm] = (prev_st[0], prev_st[1])

    def emit_prep_tables():
        s1 = sc.step("scalar", lambda s: s.activation(lngrow[:], gamrow[:], AF.Ln),
                     waits=[cl], inc="Sc")
        s2 = sc.step("scalar", lambda s: s.activation(lngcol16[:], lngcol16[:], AF.Ln),
                     waits=[cl], inc="Sc")
        m1 = sc.step("tensor", lambda t: t.matmul(ps_h[:, 0:H], onesrow_f_sb[:],
                                                  lngrow[:], start=True, stop=True),
                     waits=[s1], inc="T")
        v1 = sc.step("vector", lambda v: v.tensor_copy(lncol[:], ps_h[:, 0:H]),
                     waits=[m1], inc="V")
        prev_ev = None
        for j in range(4):
            def fmm(t, j=j):
                return t.matmul(ps_r45[0:1, 0:512], lngcol16[:],
                                hrep_sb[:, j * 512:(j + 1) * 512], start=True, stop=True)
            mmj = sc.step("tensor", fmm, waits=[s2, cl] + ([prev_ev] if prev_ev else []),
                          inc="T")
            def fev2(v, j=j):
                return v.tensor_copy(lngexp[:, j * 512:(j + 1) * 512], ps_r45[0:1, 0:512])
            prev_ev = sc.step("vector", fev2, waits=[mmj], inc="V")
        lgdone = prev_ev
        vr1 = sc.step("vector", lambda v: v.tensor_tensor(r1row, lngexp, tpow_sb,
                                                          ALU.mult),
                      waits=[lgdone], inc="V")
        sr1 = sc.step("scalar", lambda s: s.activation(r1row, r1row, AF.Exp),
                      waits=[vr1], inc="Sc")
        sgc = sc.step("scalar", lambda s: s.activation(gccol[:], lncol[:], AF.Exp,
                                                       scale=float(C)),
                      waits=[v1], inc="Sc")
        abc_mm_last = None
        for dst, row, dep in [(abc, r1row, sr1)]:
            for j in range(4):
                def fmm(t, row=row, j=j):
                    return t.matmul(ps_r45[:, 512:1024], onesrow_f_sb[:],
                                    row[:, j * 512:(j + 1) * 512], start=True, stop=True)
                abc_mm_last = sc.step("tensor", fmm, waits=[dep, sc.at("V")], inc="T")
                def fev3(v, dst=dst, j=j):
                    return v.tensor_copy(dst[:, j * 512:(j + 1) * 512], ps_r45[:, 512:1024])
                sc.step("vector", fev3, waits=[abc_mm_last], inc="V")
        for h in range(H):
            sc.step("vector", lambda v, h=h: v.tensor_scalar(
                gcbc[:, h * 128:(h + 1) * 128], ones128f_sb[:], gccol[:, h:h + 1],
                None, ALU.mult), waits=[sgc], inc="V")
        abc_done = sc.at("V")
        prev_bb = None
        for h in range(H):
            vrcp = sc.step("vector", lambda v, h=h: v.reciprocal(
                dtmp[:], abc[:, h * 128:(h + 1) * 128]),
                waits=[abc_done] if h == 0 else [prev_bb], inc="V")
            prev_bb = sc.step("vector", lambda v, h=h: v.tensor_scalar(
                bbcT[:, h * 128:(h + 1) * 128], dtmp[:],
                gccol[:, h:h + 1], None, ALU.mult), waits=[vrcp, sgc], inc="V")
        for h in range(H):
            va = sc.step("vector", lambda v, h=h: v.tensor_scalar(
                dtmp[:], expnT_sb[:], lncol[:, h:h + 1], None, ALU.mult),
                waits=[v1, cl, sc.at("Sc"), prev_bb], inc="V")
            sc.step("scalar", lambda s, h=h: s.activation(
                dmask[:, h * 128:(h + 1) * 128], dtmp[:], AF.Exp), waits=[va], inc="Sc")
        return [abc_mm_last, sr1]

    def emit_ln_pass(src_ap, dst_ap, scol0, first_waits=(), tile_waits=None):
        stores = []
        applies = []
        for i in range(NTT):
            bi = i % 2
            w = list(first_waits) if i == 0 else []
            if i >= 2:
                w.append(applies[i - 2])
            if tile_waits is not None:
                w.append(tile_waits[i])
            ld = sc.dma("sync", f"lx{bi}", xbuf[bi][:],
                        src_ap[i * 128:(i + 1) * 128, :], waits=w)
            c0 = scol0 + (i % 4) * 8
            sq = sc.step("scalar", lambda s, bi=bi, c0=c0: s.activation(
                sjunk[:], xbuf[bi][:], AF.Square, accum_out=stats[:, c0 + 1:c0 + 2]),
                waits=[ld] + ([applies[i - 4]] if i >= 4 else []), inc="Sc")
            rv = sc.step("vector", lambda v, bi=bi, c0=c0: v.reduce_sum(
                stats[:, c0:c0 + 1], xbuf[bi][:], axis=AX),
                waits=[ld] + ([applies[i - 4]] if i >= 4 else []), inc="V")
            vmu = sc.step("vector", lambda v, c0=c0: v.tensor_scalar(
                stats[:, c0 + 2:c0 + 3], stats[:, c0:c0 + 1], 1.0 / D, None, ALU.mult),
                waits=[rv], inc="V")
            vm2_ = sc.step("vector", lambda v, c0=c0: v.tensor_tensor(
                stats[:, c0 + 3:c0 + 4], stats[:, c0 + 2:c0 + 3],
                stats[:, c0 + 2:c0 + 3], ALU.mult), waits=[vmu], inc="V")
            vva = sc.step("vector", lambda v, c0=c0: v.scalar_tensor_tensor(
                stats[:, c0 + 4:c0 + 5], stats[:, c0 + 1:c0 + 2], 1.0 / D,
                stats[:, c0 + 3:c0 + 4], ALU.mult, ALU.subtract),
                waits=[vm2_, sq], inc="V")
            sl = sc.step("scalar", lambda s, c0=c0: s.activation(
                stats[:, c0 + 5:c0 + 6], stats[:, c0 + 4:c0 + 5], AF.Ln, bias=EPS),
                waits=[vva], inc="Sc")
            se = sc.step("scalar", lambda s, c0=c0: s.activation(
                stats[:, c0 + 6:c0 + 7], stats[:, c0 + 5:c0 + 6], AF.Exp, scale=-0.5),
                waits=[sl], inc="Sc")
            ob = i % 2
            ap_w = [se]
            if i >= 2:
                ap_w.append(stores[i - 2])
            vap = sc.step("vector", lambda v, bi=bi, ob=ob, c0=c0: v.tensor_scalar(
                xtout[ob][:], xbuf[bi][:], stats[:, c0 + 2:c0 + 3],
                stats[:, c0 + 6:c0 + 7], ALU.subtract, ALU.mult),
                waits=ap_w, inc="V")
            applies.append(vap)
            stores.append(sc.dma("scalar", f"sx{ob}", dst_ap[i * 128:(i + 1) * 128, :],
                                 xtout[ob][:], waits=[vap]))
        return stores

    def emit_ffn_pass(xt_ap, res_ap, out_ap, xt_stores, res_tile_waits=None,
                      ln_dst=None, scol0=64, entry_waits=()):
        out_stores = [None] * NTT
        ln_stores = [None] * NTT if ln_dst is not None else None
        ln_applies = [None] * NTT
        qsc0 = [sc.at("sx0"), sc.at("sx1")]
        hev_last = sc.at("V")
        sel_last_by_chunk = []
        r45_free = [sc.at("V"), sc.at("V")]
        sel_free = [sc.at("V"), sc.at("V")]
        mw_ev_prev = [sc.at("V")] * 4
        me_mm_by_cb = [None, None]
        res_applies = []
        hev_by_chunk = []
        for tg in range(NG):
            tla = tlb = None
            for kt in range(DT):
                w = list(entry_waits)
                if xt_stores is not None:
                    w.append(xt_stores[tg * 8 + 7])
                    w.append(xt_stores[tg * 8 + 6])
                if tg > 0:
                    w.append(sc.at("T"))
                q, clsn = ("sync", "tl") if kt < 8 else ("sync", "tlb")
                t_ = sc.dmat(q, clsn, xTg[:, kt * GL:(kt + 1) * GL],
                             xt_ap[tg * GL:(tg + 1) * GL, kt * 128:(kt + 1) * 128],
                             waits=w)
                if kt < 8:
                    tla = t_
                else:
                    tlb = t_
            for ch in range(GL // 512):
                gch = tg * (GL // 512) + ch
                cb = gch % 2
                t0 = tg * GL + ch * 512
                w = [tla, tlb, r45_free[0]] + list(entry_waits)
                def frt(t, ch=ch):
                    last = None
                    for tt in range(4):
                        L = ps_r45[:, tt * 128:tt * 128 + NE]
                        t.matmul(L, onesrow_b_sb[:], brrow[:],
                                 start=(tt == 0), stop=False)
                        for kt in range(DT):
                            last = t.matmul(
                                L, xTg[:, kt * GL + ch * 512 + tt * 128:
                                       kt * GL + ch * 512 + (tt + 1) * 128],
                                wrbuf[:, kt * NE:(kt + 1) * NE],
                                start=False, stop=(tt == 3 and kt == DT - 1))
                    return last
                rmm = sc.step("tensor", frt, waits=w, inc="T")
                vmw_l, vmk_l, sex_l = [], [], []
                for tt in range(4):
                    L = ps_r45[:, tt * 128:tt * 128 + NE]
                    c0 = 96 + ((gch % 2) * 4 + tt) * 4
                    vmx = sc.step("vector", lambda v, L=L, c0=c0: v.reduce_max(
                        stats[:, c0:c0 + 1], L, axis=AX, negate=True),
                        waits=[rmm], inc="V")
                    sex = sc.step("scalar", lambda s, L=L, tt=tt, c0=c0: s.activation(
                        etmp[tt][:], L, AF.Exp, bias=stats[:, c0:c0 + 1]),
                        waits=[vmx], inc="Sc")
                    vsm = sc.step("vector", lambda v, tt=tt, c0=c0: v.reduce_sum(
                        stats[:, c0 + 1:c0 + 2], etmp[tt][:], axis=AX),
                        waits=[sex], inc="V")
                    vrc = sc.step("vector", lambda v, c0=c0: v.reciprocal(
                        stats[:, c0 + 2:c0 + 3], stats[:, c0 + 1:c0 + 2]),
                        waits=[vsm], inc="V")
                    vsc = sc.step("vector", lambda v, c0=c0: v.tensor_scalar(
                        stats[:, c0 + 3:c0 + 4], stats[:, c0 + 2:c0 + 3],
                        FSCALE, None, ALU.mult), waits=[vrc], inc="V")
                    vmk = sc.step("vector", lambda v, L=L, tt=tt, c0=c0: v.tensor_scalar(
                        mtmp[tt][:], L, stats[:, c0:c0 + 1], 0.0, ALU.add,
                        ALU.is_equal), waits=[vsc], inc="V")
                    vmw = sc.step("vector", lambda v, tt=tt, c0=c0: v.tensor_scalar(
                        mwtile[tt][:], mtmp[tt][:], stats[:, c0 + 3:c0 + 4],
                        None, ALU.mult), waits=[vmk, mw_ev_prev[tt]], inc="V")
                    vmw_l.append(vmw)
                    vmk_l.append(vmk)
                    sex_l.append(sex)
                w2 = [vmw_l[3], sel_free[0]]
                def ftmw(t):
                    last = None
                    for tt in range(4):
                        last = t.matmul(ps_sel[0:8, tt * 128:(tt + 1) * 128],
                                        mwtile[tt][:], identf_sb[:],
                                        start=(tt == 0), stop=(tt == 3))
                    return last
                tmwg = sc.step("tensor", ftmw, waits=w2, inc="T")
                for tt in range(4):
                    mw_ev_prev[tt] = tmwg
                evw = [tmwg]
                if me_mm_by_cb[cb] is not None:
                    evw.append(me_mm_by_cb[cb])
                ev = sc.step("scalar", lambda s, cb=cb: s.copy(
                    mwT[cb][:], ps_sel[0:8, 0:512]), waits=evw, inc="Sc")
                me_evs = []
                me_banks = [ps_r45[:, 512:1024], ps_r45[:, 0:512]]
                for rp in range(4):
                    bk_ = rp % 2
                    w = [ev]
                    if bk_ == 1:
                        w += [vmk_l[3], sex_l[3]]
                    else:
                        w.append(r45_free[1])
                    if rp >= 2:
                        w.append(me_evs[rp - 2])
                    def fme(t, rp=rp, bk_=bk_, cb=cb):
                        return t.matmul(me_banks[bk_],
                                        esel_sb[:, rp * 128:(rp + 1) * 128],
                                        mwT[cb][:], start=True, stop=True)
                    mm = sc.step("tensor", fme, waits=w, inc="T")
                    if rp == 3:
                        me_mm_by_cb[cb] = mm
                    me_evs.append(sc.step("vector", lambda v, rp=rp, bk_=bk_, cb=cb:
                                          v.tensor_copy(
                                              mexp[cb][:, rp * 512:(rp + 1) * 512],
                                              me_banks[bk_]),
                                          waits=[mm], inc="V"))
                r45_free = [me_evs[3], me_evs[2]]
                w = [tla, tlb, hev_last] + list(entry_waits)
                def fh(t, ch=ch):
                    last = None
                    for mt in range(4):
                        for kt in range(DT):
                            last = t.matmul(
                                ps_h[:, mt * 512:(mt + 1) * 512],
                                abuf[:, kt * RA + mt * 128:kt * RA + (mt + 1) * 128],
                                xTg[:, kt * GL + ch * 512:kt * GL + (ch + 1) * 512],
                                start=(kt == 0), stop=(kt == DT - 1))
                    return last
                hmm = sc.step("tensor", fh, waits=w, inc="T")
                hev = None
                for mt in range(4):
                    w = [hmm, me_evs[mt]]
                    if len(sel_last_by_chunk) >= 2:
                        w.append(sel_last_by_chunk[-2])
                    hev = sc.step("vector", lambda v, mt=mt, cb=cb: v.tensor_tensor(
                        hbuf[cb][:, mt * 512:(mt + 1) * 512],
                        ps_h[:, mt * 512:(mt + 1) * 512],
                        mexp[cb][:, mt * 512:(mt + 1) * 512], ALU.mult),
                        waits=w, inc="V")
                hev_last = hev
                hev_by_chunk.append(hev)
                for tt in range(4):
                    i = (t0 // 128) + tt
                    rb = i % 2
                    w = []
                    if len(res_applies) >= 2:
                        w.append(res_applies[-2])
                    else:
                        w.extend(entry_waits)
                    if res_tile_waits is not None:
                        w.append(res_tile_waits[i])
                    rld = sc.dma("sync", f"rx{rb}", xres[rb][:],
                                 res_ap[i * 128:(i + 1) * 128, :], waits=w)
                    ob = i % 2
                    qevs = []
                    for oq in range(4):
                        bk_ = oq % 2
                        w = [hev, sel_free[bk_]]
                        if oq >= 2:
                            w.append(qevs[oq - 2])
                        def fsel(t, tt=tt, oq=oq, bk_=bk_, cb=cb):
                            last = None
                            for rp in range(4):
                                last = t.matmul(
                                    ps_sel[:, bk_ * 512:(bk_ + 1) * 512],
                                    hbuf[cb][:, rp * 512 + tt * 128:
                                              rp * 512 + (tt + 1) * 128],
                                    bbuf[:, rp * D + oq * 512:rp * D + (oq + 1) * 512],
                                    start=(rp == 0), stop=(rp == 3))
                            return last
                        smm = sc.step("tensor", fsel, waits=w, inc="T")
                        wq_ = [smm, rld]
                        if i >= 2 and out_stores[i - 2]:
                            wq_.append(out_stores[i - 2])
                        elif i < 2:
                            wq_.extend(entry_waits)
                        if ln_dst is not None and i >= 2 and ln_applies[i - 2]:
                            wq_.append(ln_applies[i - 2])
                        qev = sc.step("vector", lambda v, oq=oq, bk_=bk_, rb=rb, ob=ob:
                                      v.tensor_tensor(
                                          oub[ob][:, oq * 512:(oq + 1) * 512],
                                          ps_sel[:, bk_ * 512:(bk_ + 1) * 512],
                                          xres[rb][:, oq * 512:(oq + 1) * 512],
                                          ALU.add),
                                      waits=wq_, inc="V")
                        qevs.append(qev)
                        sel_free[bk_] = qev
                    sel_last_by_chunk.append(smm) if tt == 3 else None
                    res_applies.append(qevs[-1])
                    ost = sc.dma("scalar", f"so{ob}", out_ap[i * 128:(i + 1) * 128, :],
                                 oub[ob][:], waits=[qevs[-1]])
                    out_stores[i] = ost
                    if ln_dst is not None:
                        c0 = scol0 + (i % 4) * 8
                        sq = sc.step("scalar", lambda s, ob=ob, c0=c0: s.activation(
                            sjunk[:], oub[ob][:], AF.Square,
                            accum_out=stats[:, c0 + 1:c0 + 2]),
                            waits=[qevs[-1]], inc="Sc")
                        rv = sc.step("vector", lambda v, ob=ob, c0=c0: v.reduce_sum(
                            stats[:, c0:c0 + 1], oub[ob][:], axis=AX),
                            waits=[qevs[-1]], inc="V")
                        vmu = sc.step("vector", lambda v, c0=c0: v.tensor_scalar(
                            stats[:, c0 + 2:c0 + 3], stats[:, c0:c0 + 1], 1.0 / D,
                            None, ALU.mult), waits=[rv], inc="V")
                        vm2_ = sc.step("vector", lambda v, c0=c0: v.tensor_tensor(
                            stats[:, c0 + 3:c0 + 4], stats[:, c0 + 2:c0 + 3],
                            stats[:, c0 + 2:c0 + 3], ALU.mult), waits=[vmu], inc="V")
                        vva = sc.step("vector", lambda v, c0=c0: v.scalar_tensor_tensor(
                            stats[:, c0 + 4:c0 + 5], stats[:, c0 + 1:c0 + 2], 1.0 / D,
                            stats[:, c0 + 3:c0 + 4], ALU.mult, ALU.subtract),
                            waits=[vm2_, sq], inc="V")
                        sl = sc.step("scalar", lambda s, c0=c0: s.activation(
                            stats[:, c0 + 5:c0 + 6], stats[:, c0 + 4:c0 + 5],
                            AF.Ln, bias=EPS), waits=[vva], inc="Sc")
                        se = sc.step("scalar", lambda s, c0=c0: s.activation(
                            stats[:, c0 + 6:c0 + 7], stats[:, c0 + 5:c0 + 6],
                            AF.Exp, scale=-0.5), waits=[sl], inc="Sc")
                        lw = [se]
                        if i >= 2 and ln_stores[i - 2]:
                            lw.append(ln_stores[i - 2])
                        else:
                            lw.extend(qsc0)
                        vap = sc.step("vector", lambda v, ob=ob, c0=c0: v.tensor_scalar(
                            xtout[ob][:], oub[ob][:], stats[:, c0 + 2:c0 + 3],
                            stats[:, c0 + 6:c0 + 7], ALU.subtract, ALU.mult),
                            waits=lw, inc="V")
                        ln_applies[i] = vap
                        ln_stores[i] = sc.dma("scalar", f"sx{ob}",
                                              ln_dst[i * 128:(i + 1) * 128, :],
                                              xtout[ob][:], waits=[vap])
        return out_stores, ln_stores

    xtg_free = emit_prep_tables()
    p1a_stores = emit_ln_pass(xs, x1t, 0, first_waits=[cl])
    setup_done = [sc.at("V"), sc.at("Sc"), sc.at("T"), sc.at("bo0"),
                  sc.at("br0"),
                  sc.at("c0"), sc.at("prep")]
    ffn_entry = [brrow_tok, prep_tok] + xtg_free
    x1_stores, x2t_stores = emit_ffn_pass(x1t, xs, x1f, p1a_stores,
                                          ln_dst=x2t, scol0=64,
                                          entry_waits=ffn_entry)

    proj_final = {}
    ro_store = {}
    if "P2" in stages:
        psh2 = [ps_r45[:], ps_sel[:]]
        w3 = {"q": wqB, "k": wkB, "v": wvB, "g": wgB}
        dst = {"q": QT, "k": KT, "v": VT, "g": GT}
        bcol = {"q": 0, "k": 1, "v": 5, "g": 2}
        p2_V0 = sc.at("V")
        prev_mm = {0: None, 1: None}
        prev_store = {0: None, 1: None}
        prev_ev = [p2_V0, p2_V0]
        cur_tl = [None]

        def emit_p2_tloads(tg):
            guard = [t for t in (prev_mm[0], prev_mm[1]) if t] or [sc.at("T")]
            tl2 = [None, None]
            for kt in range(DT):
                w = [x2t_stores[tg * 8 + 7], x2t_stores[tg * 8 + 6]] + guard
                q, clsn, slot = ("sync", "tl", 0) if kt < 8 else ("sync", "tlb", 1)
                tl2[slot] = sc.dmat(q, clsn, xTg[:, kt * GL:(kt + 1) * GL],
                                    x2t[tg * GL:(tg + 1) * GL,
                                        kt * 128:(kt + 1) * 128],
                                    waits=w)
            cur_tl[0] = tl2

        def emit_p2_m(nm, tg, m):
            tl2 = cur_tl[0]
            b0 = tg // 2
            ch0 = (tg % 2) * 8
            wB3 = w3[nm].rearrange("(a p) d -> p a d", p=128)
            gi = m % 2
            wwt = list(wprep[nm])
            if prev_mm[gi] is not None:
                wwt.append(prev_mm[gi])
            wld = sc.dma("sync", f"wt{gi}",
                         wtile[gi].rearrange("p (a d) -> p a d", d=128),
                         wB3[:, :, m * 128:(m + 1) * 128], waits=wwt)
            hb = m % 2
            def fp2(t, gi=gi, hb=hb):
                last = None
                for kt in range(DT):
                    for n2 in range(2):
                        last = t.matmul(
                            psh2[hb][:, n2 * 512:(n2 + 1) * 512],
                            wtile[gi][:, kt * 128:(kt + 1) * 128],
                            xTg[:, kt * GL + n2 * 512:kt * GL + (n2 + 1) * 512],
                            start=(kt == 0), stop=(kt == DT - 1))
                return last
            mm = sc.step("tensor", fp2, waits=tl2 + [wld, prev_ev[hb]], inc="T")
            prev_mm[gi] = mm
            sp = m % 2
            wev = [mm]
            if prev_store[sp] is not None:
                wev.append(prev_store[sp])
            if nm == "g":
                ev = sc.step("scalar", lambda s, m=m, hb=hb, sp=sp: s.activation(
                    qkstage[sp][:], psh2[hb][:], AF.Silu,
                    bias=biases[:, bcol["g"] * DT + m:bcol["g"] * DT + m + 1]),
                    waits=wev, inc="Sc")
            else:
                ev = sc.step("vector", lambda s, m=m, hb=hb, sp=sp, nm=nm:
                             s.tensor_scalar(
                                 qkstage[sp][:], psh2[hb][:],
                                 biases[:, bcol[nm] * DT + m:bcol[nm] * DT + m + 1],
                                 None, ALU.add), waits=wev, inc="V")
            prev_ev[hb] = ev
            st = sc.dma("scalar", f"sqk{sp}",
                        dst[nm][b0, ch0:ch0 + 8, m].rearrange("c p t -> p c t"),
                        qkstage[sp].rearrange("p (c t) -> p c t", t=C),
                        waits=[ev])
            prev_store[sp] = st
            if m == DT - 1:
                proj_final[(nm, tg)] = (prev_store[0], prev_store[1])

    if "P3" in stages:
        xbufB = [xb.bitcast(BF16) for xb in xbuf]
        xresB = [xr.bitcast(BF16) for xr in xres]
        qch = [xbufB[0][:, 0:D], xbufB[0][:, D:2 * D]]
        kch = [xbufB[1][:, 0:D], xbufB[1][:, D:2 * D]]
        vtch = [xresB[0][:, 0:D], xresB[0][:, D:2 * D]]
        vtok = [xresB[1][:, 0:D], xresB[1][:, D:2 * D]]
        khatT = [hbuf[0][:], hbuf[1][:]]
        khtok = [mexp[0][:], mexp[1][:]]
        smask = [WSTG[0][:, 0:D], WSTG[0][:, D:2 * D]]
        retsb = [WSTG[1][:, 0:D], WSTG[1][:, D:2 * D]]
        qhat = qch
        st_f = [oub[0], oub[1]]
        st_b = [xtout[0], xtout[1]]
        statacc = stats[:, 128:192]

        p3v0 = sc.at("V")
        p3s0 = sc.at("Sc")
        mz = sc.step("vector", lambda v: v.memset(statacc, 0.0), waits=[p3v0],
                     inc="V")
        stz = [sc.step("vector", lambda v, b=b: v.memset(st_f[b][:], 0.0),
                       waits=[p3v0, sc.at("so0"), sc.at("so1")], inc="V")
               for b in range(NB)]
        t3 = dict(ld_gate=[(), ()], sppr=[None, None],
                  psh_free=[sc.at("Sc"), sc.at("V"), p3v0],
                  prev_cast=[None, None], prev_sro=[None, None],
                  prev_stats=None, prev_gsm=None, prev_wst=[(), ()])

        def p3_unit_gen(c, b):
            it = c * 2 + b
            cb = it % 2
            pf = lambda nm_: list(proj_final.get((nm_, b * 2 + c // 8), ()))
            gate_ = list(t3["ld_gate"][cb])
            r3 = lambda ap: ap.rearrange("dt p t -> p dt t")
            s3 = lambda ap: ap.rearrange("p (dt t) -> p dt t", t=C)
            lq = sc.dma("gpsimd", f"q3{cb}", s3(qch[cb]), r3(QT[b, c]),
                        waits=pf("q") + gate_)
            lk = sc.dma("gpsimd", f"k3{cb}", s3(kch[cb]), r3(KT[b, c]),
                        waits=pf("k") + gate_)
            lv = sc.dma("gpsimd", f"v3{cb}", s3(vtch[cb]), r3(VT[b, c]),
                        waits=pf("v") + gate_)
            wg_ = [t3["sppr"][cb]] if t3["sppr"][cb] else []
            if it < 2:
                wg_ = wg_ + [setup_done[0]]
            def fs2(t, cb=cb):
                last = None
                for h in range(H):
                    last = t.matmul(ps_h[:, h * 128:(h + 1) * 128],
                                    kch[cb][:, h * 128:(h + 1) * 128],
                                    qch[cb][:, h * 128:(h + 1) * 128],
                                    start=(h % 4 == 0), stop=(h % 4 == 3))
                return last
            s2mm = sc.step("tensor", fs2, waits=[lq, lk] + t3["psh_free"], inc="T")
            gqh = sc.step("gpsimd", lambda g, cb=cb: g.tensor_tensor(
                qhat[cb], qch[cb], abc[:], ALU.mult), waits=[s2mm] + wg_, inc="G")
            gkh = sc.step("gpsimd", lambda g, cb=cb: g.tensor_tensor(
                khatT[cb], kch[cb], bbcT[:], ALU.mult), waits=[s2mm, lk] + wg_,
                inc="G")
            yield
            wsc = [s2mm]
            if t3["prev_gsm"] is not None:
                wsc.append(t3["prev_gsm"])
            scp = sc.step("scalar", lambda s: s.copy(sjunk[:], ps_h[:]),
                          waits=wsc, inc="Sc")
            wsm = [scp] + ([t3["sppr"][cb]] if t3["sppr"][cb] else [])
            if it < 2:
                wsm += [wprep["o"][0], wprep["o"][1]]
            gsm = sc.step("gpsimd", lambda g, cb=cb: g.tensor_tensor(
                smask[cb], sjunk[:], dmask[:], ALU.mult), waits=wsm, inc="G")
            t3["prev_gsm"] = gsm
            trk = sc.step("tensor", lambda t, cb=cb: _trmm(t, khatT[cb]),
                          waits=[gkh, scp], inc="T")
            ekt = sc.step("vector", lambda v, cb=cb: v.tensor_copy(
                khtok[cb], ps_h[:]),
                waits=[trk] + ([t3["sppr"][cb]] if t3["sppr"][cb] else []), inc="V")
            yield
            trv = sc.step("tensor", lambda t, cb=cb: _trmm(t, vtch[cb]),
                          waits=[lv, ekt], inc="T")
            evt = sc.step("scalar", lambda s, cb=cb: s.copy(vtok[cb], ps_h[:]),
                          waits=[trv] + ([t3["sppr"][cb]] if t3["sppr"][cb] else []),
                          inc="Sc")
            yield
            def fret(t, cb=cb, b=b, c=c):
                last = None
                for h in range(H):
                    last = t.matmul(ps_h[:, h * 128:(h + 1) * 128],
                                    vtok[cb][:, h * 128:(h + 1) * 128],
                                    smask[cb][:, h * 128:(h + 1) * 128],
                                    start=(h % 4 == 0),
                                    stop=(h % 4 == 3 and c == 0))
                    if c > 0:
                        last = t.matmul(ps_h[:, h * 128:(h + 1) * 128],
                                        st_b[b][:, h * 128:(h + 1) * 128],
                                        qhat[cb][:, h * 128:(h + 1) * 128],
                                        start=False, stop=(h % 4 == 3))
                return last
            wret = [evt, gsm, gqh]
            if c > 0:
                wret.append(t3["prev_cast"][b])
            rmm3 = sc.step("tensor", fret, waits=wret, inc="T")
            wer = [rmm3]
            if it < 2:
                wer += [wprep["o"][0], wprep["o"][1]]
            if t3["prev_sro"][cb] is not None:
                wer.append(t3["prev_sro"][cb])
            er1 = sc.step("scalar", lambda s, cb=cb: s.copy(
                retsb[cb][:, 0:1024], ps_h[:, 0:1024]), waits=wer, inc="Sc")
            er2 = sc.step("scalar", lambda s, cb=cb: s.copy(
                retsb[cb][:, 1024:2048], ps_h[:, 1024:2048]), waits=[rmm3],
                inc="Sc")
            sro = sc.dma("gpsimd", f"sro{cb}",
                         ROT[b, c].rearrange("dt p t -> p dt t"),
                         retsb[cb].rearrange("p (dt t) -> p dt t", t=C),
                         waits=[er2, er1] +
                         ([t3["prev_sro"][cb]] if t3["prev_sro"][cb] else []))
            t3["prev_sro"][cb] = sro
            pstat = t3["prev_stats"]
            vst1 = sc.step("vector", lambda v, cb=cb: v.reduce_sum(
                stats[:, 224:240], retsb[cb].rearrange("p (s t) -> p s t", s=16),
                axis=AX), waits=[er1, er2] + ([pstat] if pstat else []), inc="V")
            vst2 = sc.step("vector", lambda v, b=b: v.tensor_tensor(
                statacc[:, b * 16:(b + 1) * 16], statacc[:, b * 16:(b + 1) * 16],
                stats[:, 224:240], ALU.add), waits=[vst1, mz], inc="V")
            def fsq(s, cb=cb):
                last = None
                for h in range(H):
                    last = s.activation(dtmp[:],
                                        retsb[cb][:, h * 128:(h + 1) * 128],
                                        AF.Square,
                                        accum_out=stats[:, 240 + h:241 + h])
                return last
            ssq = sc.step("scalar", fsq, waits=[er1, er2] +
                          ([pstat] if pstat else []), inc="Sc")
            vst4 = sc.step("vector", lambda v, b=b: v.tensor_tensor(
                statacc[:, 32 + b * 16:32 + (b + 1) * 16],
                statacc[:, 32 + b * 16:32 + (b + 1) * 16],
                stats[:, 240:256], ALU.add), waits=[ssq, vst2], inc="V")
            t3["prev_stats"] = vst4
            yield
            def fds(t, cb=cb):
                last = None
                for h in range(H):
                    last = t.matmul(ps_h[:, h * 128:(h + 1) * 128],
                                    khtok[cb][:, h * 128:(h + 1) * 128],
                                    vtok[cb][:, h * 128:(h + 1) * 128],
                                    start=(h % 4 == 0), stop=(h % 4 == 3))
                return last
            dsm = sc.step("tensor", fds, waits=[ekt, evt, er1, er2], inc="T")
            t3["sppr"][cb] = dsm
            wd = [stz[b]] + list(t3["prev_wst"][b])
            if t3["prev_cast"][b] is not None:
                wd.append(t3["prev_cast"][b])
            wst1 = sc.step("vector", lambda v, b=b: v.tensor_tensor(
                st_f[b][:], st_f[b][:], gcbc[:], ALU.mult), waits=wd, inc="V")
            wst2 = sc.step("vector", lambda v, b=b: v.tensor_tensor(
                st_f[b][:, 0:1024], st_f[b][:, 0:1024], ps_h[:, 0:1024], ALU.add),
                waits=[dsm, wst1], inc="V")
            wst3 = sc.step("vector", lambda v, b=b: v.tensor_tensor(
                st_f[b][:, 1024:2048], st_f[b][:, 1024:2048], ps_h[:, 1024:2048],
                ALU.add), waits=[dsm, wst1], inc="V")
            t3["prev_wst"][b] = (wst2, wst3)
            t3["psh_free"] = [wst2, wst3]
            cast = sc.step("scalar", lambda s, b=b: s.copy(st_b[b][:], st_f[b][:]),
                           waits=[wst2, wst3], inc="Sc")
            t3["prev_cast"][b] = cast
            t3["ld_gate"][cb] = (rmm3, gqh, gkh, gsm, dsm)

    if "P2" in stages:
        aorder = [0, 2, 1, 3]
        p3_ready = {}
        for i, tg in enumerate(aorder):
            b_ = tg // 2
            for c_ in range((tg % 2) * 8, (tg % 2) * 8 + 8):
                p3_ready[(c_, b_)] = i + 1
        p3_list = [(c_, b_) for c_ in range(NCH) for b_ in range(NB)]
        p3_state = {"idx": 0, "gen": None}

        def pump_p3(done_sub, credits):
            if "P3" not in stages:
                return
            n = 0
            while n < credits:
                if p3_state["gen"] is None:
                    if (p3_state["idx"] >= len(p3_list) or
                            p3_ready[p3_list[p3_state["idx"]]] > done_sub):
                        return
                    c_, b_ = p3_list[p3_state["idx"]]
                    p3_state["gen"] = p3_unit_gen(c_, b_)
                try:
                    next(p3_state["gen"])
                except StopIteration:
                    p3_state["gen"] = None
                    p3_state["idx"] += 1
                n += 1

        for i, tg in enumerate(aorder):
            emit_p2_tloads(tg)
            pump_p3(i, 4)
            for nm in ["q", "k", "v"]:
                for m in range(DT):
                    emit_p2_m(nm, tg, m)
                    pump_p3(i, 1)
        for i, tg in enumerate(aorder):
            emit_p2_tloads(tg)
            pump_p3(4, 4)
            for m in range(DT):
                emit_p2_m("g", tg, m)
                pump_p3(4, 2)
        pump_p3(4, 400)

    if "P3" in stages:
        gn_done = []
        for b in range(NB):
            def fgn(t, b=b):
                t.matmul(ps_r45[0:1, 0:16], onescol_f_sb[:],
                         statacc[:, b * 16:(b + 1) * 16], start=True, stop=False)
                return t.matmul(ps_r45[0:1, 16:32], onescol_f_sb[:],
                                statacc[:, 32 + b * 16:32 + (b + 1) * 16],
                                start=False, stop=True)
            gmm = sc.step("tensor", fgn,
                          waits=[t3["prev_stats"], prev_ev[0], prev_ev[1],
                                 t3["psh_free"][0], t3["psh_free"][1]] + gn_done,
                          inc="T")
            NSD = float(S * 128)
            g1 = sc.step("vector", lambda v: v.tensor_scalar(
                stats[0:1, 96:112], ps_r45[0:1, 0:16], 1.0 / NSD, None, ALU.mult),
                waits=[gmm], inc="V")
            g2 = sc.step("vector", lambda v: v.tensor_scalar(
                stats[0:1, 112:128], ps_r45[0:1, 16:32], 1.0 / NSD, None,
                ALU.mult), waits=[g1], inc="V")
            g3 = sc.step("vector", lambda v: v.tensor_tensor(
                stats[0:1, 240:256], stats[0:1, 96:112], stats[0:1, 96:112],
                ALU.mult), waits=[g2], inc="V")
            g4 = sc.step("vector", lambda v: v.tensor_tensor(
                stats[0:1, 112:128], stats[0:1, 112:128], stats[0:1, 240:256],
                ALU.subtract), waits=[g3], inc="V")
            g5 = sc.step("scalar", lambda s: s.activation(
                stats[0:1, 240:256], stats[0:1, 112:128], AF.Ln, bias=EPS),
                waits=[g4], inc="Sc")
            g6 = sc.step("scalar", lambda s: s.activation(
                stats[0:1, 112:128], stats[0:1, 240:256], AF.Exp, scale=-0.5),
                waits=[g5], inc="Sc")
            def fbc(t, b=b):
                t.matmul(ps_r45[:, 512:528], onesrow_f_sb[:],
                         stats[0:1, 112:128], start=True, stop=False)
                return t.matmul(ps_r45[:, 528:544], onesrow_f_sb[:],
                                stats[0:1, 96:112], start=False, stop=True)
            bmm = sc.step("tensor", fbc, waits=[g6], inc="T")
            g8 = sc.step("vector", lambda v, b=b: v.tensor_tensor(
                stats[:, 192 + b * 16:192 + (b + 1) * 16], biases[:, 3 * DT:4 * DT],
                ps_r45[:, 512:528], ALU.mult), waits=[bmm], inc="V")
            g9 = sc.step("vector", lambda v, b=b: v.tensor_tensor(
                stats[:, 240:256], stats[:, 192 + b * 16:192 + (b + 1) * 16],
                ps_r45[:, 528:544], ALU.mult), waits=[g8], inc="V")
            g10 = sc.step("vector", lambda v, b=b: v.tensor_tensor(
                stats[:, 224 + b * 16:224 + (b + 1) * 16], biases[:, 4 * DT:5 * DT],
                stats[:, 240:256], ALU.subtract), waits=[g9], inc="V")
            gn_done.append(g10)

    x3_stores = [None] * NTT
    x4t_stores = [None] * NTT
    if "P4" in stages:
        wo_home = ([xTg[:, kt * D:(kt + 1) * D] for kt in range(8)] +
                   [wtile[0][:], wtile[1][:], WSTG[1][:, 0:D], WSTG[1][:, D:2 * D],
                    None,
                    WSTG[0][:, D:2 * D], mexp[0][:], mexp[1][:]])

        def wo_ap(kt, oq):
            if kt == 12:
                return qkstage[oq // 2][:, (oq % 2) * 512:(oq % 2) * 512 + 512]
            return wo_home[kt][:, oq * 512:(oq + 1) * 512]

        wo_gate = ([sc.at("T"), sc.at("Sc"), sc.at("V"), wprep["o"][0],
                    wprep["o"][1], t3["prev_sro"][0], t3["prev_sro"][1]] +
                   list(proj_final.get(("g", 3), ())))
        wo_lds = []
        for kt in range(DT):
            if kt == 12:
                wo_lds.append(sc.dma("sync", "wo5", qkstage[0][:],
                                     woB[kt * 128:(kt + 1) * 128, 0:1024],
                                     waits=wo_gate))
                wo_lds.append(sc.dma("sync", "wo5", qkstage[1][:],
                                     woB[kt * 128:(kt + 1) * 128, 1024:2048],
                                     waits=wo_gate))
            else:
                wo_lds.append(sc.dma("sync", "wo5", wo_home[kt],
                                     woB[kt * 128:(kt + 1) * 128, :],
                                     waits=wo_gate))
        roch = [xbufB[0][:, 0:D], xbufB[0][:, D:2 * D]]
        gch2 = [xbufB[1][:, 0:D], xbufB[1][:, D:2 * D]]
        rogate = [hbuf[0][:], hbuf[1][:]]
        x1res = [xres[0], xres[1]]
        outb = [oub[0], oub[1]]
        p4v0 = sc.at("V")
        p4t0 = sc.at("T")
        p4s0 = sc.at("Sc")
        p4_psum = [(ps_r45[:], ps_sel[:]), (ps_h[:, 0:1024], ps_h[:, 1024:2048])]
        prev_ld4 = [None, None]
        prev_x1 = [None, None]
        prev_ps_ev = [[p4v0, p4v0], [p4v0, p4v0]]
        prev_rg = [None, None]
        prev_oub_st = [None, None]
        prev_ln_ap = [None, None]
        prev_ln_st = [None, None]
        p4_iters = [(b, c) for b in range(NB) for c in range(NCH)]
        preps = {}
        lx1s = {}

        def p4_prep(i):
            b, c = p4_iters[i]
            cb = i % 2
            wro = [t3["prev_sro"][0], t3["prev_sro"][1], gn_done[b]]
            if c == 0:
                wro.append(sc.at("G"))
            if prev_ld4[cb] is not None:
                wro.append(prev_ld4[cb])
            r3 = lambda ap: ap.rearrange("dt p t -> p dt t")
            s3 = lambda ap: ap.rearrange("p (dt t) -> p dt t", t=C)
            lro = sc.dma("gpsimd", f"ro4{cb}", s3(roch[cb]), r3(ROT[b, c]),
                         waits=wro)
            lg = sc.dma("gpsimd", f"g4{cb}", s3(gch2[cb]), r3(GT[b, c]),
                        waits=list(proj_final.get(("g", b * 2 + c // 8), ())) +
                              ([prev_ld4[cb]] if prev_ld4[cb] else []))
            lx1s[i] = sc.dma("sync", f"rx4{cb}", x1res[cb][:],
                             x1f[i * 128:(i + 1) * 128, :],
                             waits=[prev_x1[cb]] if prev_x1[cb] else [])
            tga = None
            for dt_ in range(DT):
                wga = [lro, gn_done[b]]
                if dt_ == 0 and prev_rg[cb] is not None:
                    wga.append(prev_rg[cb])
                tga = sc.step("vector", lambda v, cb=cb, b=b, dt_=dt_: v.tensor_scalar(
                    sjunk[:, dt_ * 128:(dt_ + 1) * 128],
                    roch[cb][:, dt_ * 128:(dt_ + 1) * 128],
                    stats[:, 192 + b * 16 + dt_:192 + b * 16 + dt_ + 1],
                    stats[:, 224 + b * 16 + dt_:224 + b * 16 + dt_ + 1],
                    ALU.mult, ALU.add), waits=wga, inc="V")
            vrg = sc.step("vector", lambda v, cb=cb: v.tensor_tensor(
                rogate[cb], sjunk[:], gch2[cb], ALU.mult), waits=[tga, lg],
                inc="V")
            prev_ld4[cb] = vrg
            preps[i] = vrg

        p4_prep(0)
        for b in range(NB):
            for c in range(NCH):
                i = b * NCH + c
                cb = i % 2
                if i + 1 < len(p4_iters):
                    p4_prep(i + 1)
                vrg = preps[i]
                lx1 = lx1s[i]
                pp = i % 2
                psA, psB2 = p4_psum[pp]
                def fop(t, cb=cb, pp=pp):
                    psA, psB2 = p4_psum[pp]
                    last = None
                    for oq in range(4):
                        tgt = psA if oq < 2 else psB2
                        col = (oq % 2) * 512
                        t.matmul(tgt[:, col:col + 512], onesrow_b_sb[:],
                                 biasrow[:, oq * 512:(oq + 1) * 512],
                                 start=True, stop=False)
                        for kt in range(DT):
                            last = t.matmul(
                                tgt[:, col:col + 512],
                                rogate[cb][:, kt * 128:(kt + 1) * 128],
                                wo_ap(kt, oq),
                                start=False, stop=(kt == DT - 1))
                    return last
                wop = [vrg, borow_tok] + prev_ps_ev[pp]
                if i == 0:
                    wop += [p4t0] + ([wo_lds[-1]] if wo_lds else [])
                opm = sc.step("tensor", fop, waits=wop, inc="T")
                prev_rg[cb] = opm
                ob4 = cb
                wev4 = [opm, lx1]
                if prev_oub_st[ob4] is not None:
                    wev4.append(prev_oub_st[ob4])
                if prev_ln_ap[ob4] is not None:
                    wev4.append(prev_ln_ap[ob4])
                e1 = sc.step("vector", lambda v, cb=cb, pp=pp: v.tensor_tensor(
                    outb[cb][:, 0:1024], p4_psum[pp][0], x1res[cb][:, 0:1024],
                    ALU.add), waits=wev4, inc="V")
                e2 = sc.step("vector", lambda v, cb=cb, pp=pp: v.tensor_tensor(
                    outb[cb][:, 1024:2048], p4_psum[pp][1], x1res[cb][:, 1024:2048],
                    ALU.add), waits=[e1], inc="V")
                prev_ps_ev[pp] = [e1, e2]
                prev_x1[cb] = e2
                st4 = sc.dma("scalar", f"so4{cb}", x3f[i * 128:(i + 1) * 128, :],
                             outb[cb][:], waits=[e2])
                x3_stores[i] = st4
                prev_oub_st[ob4] = st4
                c0 = 64 + (i % 4) * 8
                sq = sc.step("scalar", lambda s, cb=cb, c0=c0: s.activation(
                    WSTG[0][:, 0:D], outb[cb][:], AF.Square,
                    accum_out=stats[:, c0 + 1:c0 + 2]), waits=[e2, vrg], inc="Sc")
                rv = sc.step("vector", lambda v, cb=cb, c0=c0: v.reduce_sum(
                    stats[:, c0:c0 + 1], outb[cb][:], axis=AX), waits=[e2], inc="V")
                vmu = sc.step("vector", lambda v, c0=c0: v.tensor_scalar(
                    stats[:, c0 + 2:c0 + 3], stats[:, c0:c0 + 1], 1.0 / D, None,
                    ALU.mult), waits=[rv], inc="V")
                vm2_ = sc.step("vector", lambda v, c0=c0: v.tensor_tensor(
                    stats[:, c0 + 3:c0 + 4], stats[:, c0 + 2:c0 + 3],
                    stats[:, c0 + 2:c0 + 3], ALU.mult), waits=[vmu], inc="V")
                vva = sc.step("vector", lambda v, c0=c0: v.scalar_tensor_tensor(
                    stats[:, c0 + 4:c0 + 5], stats[:, c0 + 1:c0 + 2], 1.0 / D,
                    stats[:, c0 + 3:c0 + 4], ALU.mult, ALU.subtract),
                    waits=[vm2_, sq], inc="V")
                sl = sc.step("scalar", lambda s, c0=c0: s.activation(
                    stats[:, c0 + 5:c0 + 6], stats[:, c0 + 4:c0 + 5], AF.Ln,
                    bias=EPS), waits=[vva], inc="Sc")
                se = sc.step("scalar", lambda s, c0=c0: s.activation(
                    stats[:, c0 + 6:c0 + 7], stats[:, c0 + 5:c0 + 6], AF.Exp,
                    scale=-0.5), waits=[sl], inc="Sc")
                lw = [se]
                if prev_ln_st[cb] is not None:
                    lw.append(prev_ln_st[cb])
                else:
                    lw += [sc.at("sx0"), sc.at("sx1")]
                vap = sc.step("vector", lambda v, cb=cb, c0=c0: v.tensor_scalar(
                    xtout[cb][:], outb[cb][:], stats[:, c0 + 2:c0 + 3],
                    stats[:, c0 + 6:c0 + 7], ALU.subtract, ALU.mult),
                    waits=lw, inc="V")
                prev_ln_ap[cb] = vap
                st5 = sc.dma("scalar", f"sx4{cb}", x4t[i * 128:(i + 1) * 128, :],
                             xtout[cb][:], waits=[vap])
                x4t_stores[i] = st5
                prev_ln_st[cb] = st5
                ld_sq_guard = sq

    if "P5" in stages:
        p5_entry = [brrow_tok, prep_tok, sc.at("T"), sc.at("V"), sc.at("Sc"),
                    sc.at("G"), x3_stores[NTT - 1], x3_stores[NTT - 2]]
        y_stores, _ = emit_ffn_pass(x4t, x3f, y, x4t_stores,
                                    res_tile_waits=x3_stores, ln_dst=None,
                                    scol0=64, entry_waits=p5_entry)
        fin = [sc.at("so0"), sc.at("so1")]
    else:
        fin = []

    drain = [sc.at(s) for s in list(sc.cnt) if s not in ("T", "V", "Sc", "G")]
    sc.step("sync", lambda e: e.nop(), waits=drain + fin)

    sc.emit_and_close()
    return nc


def _consts():
    t = np.arange(128, dtype=np.float32)
    expnT_ = t[None, :] - t[:, None]
    expnT_[expnT_ < 0] = 1.0e6
    tpow_ = np.tile(t + 1.0, H)[None, :]
    HREP_ = np.zeros((H, D), dtype=np.float32)
    for h in range(H):
        HREP_[h, h * 128:(h + 1) * 128] = 1.0
    Esel_ = np.zeros((NE, RA), dtype=np.float32)
    for e in range(NE):
        Esel_[e, e * RANK:(e + 1) * RANK] = 1.0
    import ml_dtypes
    bf = lambda a: a.astype(ml_dtypes.bfloat16)
    return dict(
        identf=np.eye(128, dtype=np.float32),
        identb=bf(np.eye(128, dtype=np.float32)),
        onesrow_b=bf(np.ones((1, 128), np.float32)),
        onesrow_f=np.ones((1, 128), np.float32),
        ones128f=np.ones((128, 128), np.float32),
        onescol_f=np.ones((128, 1), np.float32),
        expnT=expnT_.astype(np.float32),
        tpow=tpow_.astype(np.float32),
        HREP=HREP_, Esel=bf(Esel_),
        cvals=np.tile(np.array([[1e-5, -0.5, 128.0, 0.5 / 64, 1.0 / 2048, 2.0, 0, 0]],
                                dtype=np.float32), (128, 1)),
    )


def make_inmaps(inputs):
    consts = _consts()
    x = np.ascontiguousarray(np.asarray(inputs["L0_attn_x"], dtype=np.float32))
    shards = np.split(x.reshape(16 * S, D), 8, axis=0)
    base = dict(
        wq=np.ascontiguousarray(inputs["L0_attn_wq"], dtype=np.float32),
        wk=np.ascontiguousarray(inputs["L0_attn_wk"], dtype=np.float32),
        wv=np.ascontiguousarray(inputs["L0_attn_wv"], dtype=np.float32),
        wg=np.ascontiguousarray(inputs["L0_attn_wg"], dtype=np.float32),
        wo=np.ascontiguousarray(inputs["L0_attn_wo"], dtype=np.float32),
        bq=np.ascontiguousarray(inputs["L0_attn_bq"], dtype=np.float32),
        bk=np.ascontiguousarray(inputs["L0_attn_bk"], dtype=np.float32),
        bv=np.ascontiguousarray(inputs["L0_attn_bv"], dtype=np.float32),
        bg=np.ascontiguousarray(inputs["L0_attn_bg"], dtype=np.float32),
        bo=np.ascontiguousarray(inputs["L0_attn_bo"], dtype=np.float32),
        gnw=np.ascontiguousarray(inputs["L0_attn_gn_w"], dtype=np.float32),
        gnb=np.ascontiguousarray(inputs["L0_attn_gn_b"], dtype=np.float32),
        gamma=np.ascontiguousarray(inputs["L0_attn_gamma"], dtype=np.float32),
        fAT=np.ascontiguousarray(
            np.asarray(inputs["L0_ffn_A"], dtype=np.float32)
            .reshape(NE, RANK, DT, 128).transpose(3, 2, 0, 1).reshape(128, DT * RA)),
        fBT=np.ascontiguousarray(
            np.asarray(inputs["L0_ffn_B"], dtype=np.float32)
            .reshape(4, 2, D, RANK).transpose(1, 3, 0, 2).reshape(128, 4 * D)),
        fWrT=np.ascontiguousarray(
            np.asarray(inputs["L0_ffn_Wr"], dtype=np.float32)
            .reshape(NE, DT, 128).transpose(2, 1, 0).reshape(128, DT * NE)),
        fbr=np.ascontiguousarray(np.asarray(inputs["L0_ffn_br"],
                                            dtype=np.float32).reshape(1, NE)),
        **consts,
    )
    return [dict(base, xs=np.ascontiguousarray(s)) for s in shards]


def kernel(**inputs):
    nc = build(stages=("P1", "P2", "P3", "P4", "P5"))
    in_maps = make_inmaps(inputs)
    res = run_bass_kernel_spmd(nc, in_maps, core_ids=list(range(8)))
    out = np.concatenate([r["y"] for r in res.results], axis=0)
    return out.reshape(16, S, D)


# version 26
# speedup vs baseline: 1.2006x; 1.2006x over previous
import contextlib
import numpy as np
import concourse.bass as bass
import concourse.mybir as mybir
from concourse.bass_utils import run_bass_kernel_spmd

F32 = mybir.dt.float32
BF16 = mybir.dt.bfloat16
ALU = mybir.AluOpType
AF = mybir.ActivationFunctionType
AX = mybir.AxisListType.X

D = 2048
DT = 16
NB = 2
S = 2048
NTOK = NB * S
NTT = NTOK // 128
H = 16
C = 128
NCH = S // C
RANK = 64
NE = 8
RA = NE * RANK
EPS = 1e-5
FSCALE = 0.5 / RANK

ENGINES = ("sync", "tensor", "vector", "scalar", "gpsimd")

STAGES = ("P1",)


class Sched:

    def __init__(self, nc):
        self.nc = nc
        self.steps = {e: [] for e in ENGINES}
        self.sems = {}
        self.cnt = {}
        self._ctx = contextlib.ExitStack()
        for s in ["T", "V", "Sc", "G"]:
            self._mksem(s)

    def _mksem(self, s):
        self.sems[s] = self._ctx.enter_context(self.nc.semaphore(s))
        self.cnt[s] = 0

    def at(self, name):
        if name not in self.cnt:
            self._mksem(name)
        return (name, self.cnt[name])

    def step(self, eng, fn, waits=(), inc=None, inc_n=1):
        w = tuple((s, v) for t in waits if t is not None for (s, v) in (t,) if v > 0)
        self.steps[eng].append((w, fn, inc, inc_n))
        if inc is not None:
            self.cnt[inc] += inc_n
            return (inc, self.cnt[inc])
        return None

    def dma(self, qeng, cls, out, in_, waits=(), **kw):
        if cls not in self.cnt:
            self._mksem(cls)
        def fn(e, out=out, in_=in_, kw=kw):
            return e.dma_start(out, in_, **kw)
        return self.step(qeng, fn, waits=waits, inc=cls, inc_n=16)

    def dmat(self, qeng, cls, out, in_, waits=()):
        if cls not in self.cnt:
            self._mksem(cls)
        def fn(e, out=out, in_=in_):
            return e.dma_start_transpose(out, in_)
        return self.step(qeng, fn, waits=waits, inc=cls, inc_n=16)

    def run(self, eng_name, eng):
        for waits, fn, inc, inc_n in self.steps[eng_name]:
            for s, v in waits:
                eng.wait_ge(self.sems[s], v)
            ins = fn(eng)
            if inc is not None:
                assert ins is not None, f"{eng_name} step with inc={inc} returned None"
                ins.then_inc(self.sems[inc], inc_n)

    def emit_and_close(self):
        with self.nc.Block() as block:
            for e in ENGINES:
                getattr(block, e)(lambda eng, e=e: self.run(e, eng))
        self._ctx.close()


def build(stages=STAGES, debug=()):
    nc = bass.Bass("TRN2", target_bir_lowering=False)

    def dram(name, shape, dtype, kind="Internal"):
        k = "ExternalOutput" if name in debug else kind
        return nc.dram_tensor(name, shape, dtype, kind=k).ap()

    EI = lambda n, s, d: nc.dram_tensor(n, s, d, kind="ExternalInput").ap()
    xs = EI("xs", [NTOK, D], F32)
    wq, wk, wv, wg, wo = (EI(n, [D, D], F32) for n in ["wq", "wk", "wv", "wg", "wo"])
    bq, bk, bv, bg, bo = (EI(n, [D], F32) for n in ["bq", "bk", "bv", "bg", "bo"])
    gnw, gnb = EI("gnw", [D], F32), EI("gnb", [D], F32)
    gamma = EI("gamma", [H], F32)
    fAT = EI("fAT", [128, DT * RA], F32)
    fBT = EI("fBT", [128, 4 * D], F32)
    fWrT = EI("fWrT", [128, DT * NE], F32)
    fbr = EI("fbr", [1, NE], F32)
    identf = EI("identf", [128, 128], F32)
    identb = EI("identb", [128, 128], BF16)
    onesrow_b = EI("onesrow_b", [1, 128], BF16)
    onesrow_f = EI("onesrow_f", [1, 128], F32)
    ones128f = EI("ones128f", [128, 128], F32)
    onescol_f = EI("onescol_f", [128, 1], F32)
    expnT = EI("expnT", [128, 128], F32)
    tpow = EI("tpow", [1, D], F32)
    HREP = EI("HREP", [H, D], F32)
    cvals = EI("cvals", [128, 8], F32)
    Esel = EI("Esel", [NE, RA], BF16)

    y = dram("y", [NTOK, D], F32, kind="ExternalOutput")
    wqB = dram("wqB", [D, D], BF16)
    wkB = dram("wkB", [D, D], BF16)
    wvB = dram("wvB", [D, D], BF16)
    wgB = dram("wgB", [D, D], BF16)
    woB = dram("woB", [D, D], BF16)
    x1t = dram("x1t", [NTOK, D], BF16)
    x1f = dram("x1f", [NTOK, D], F32)
    x2t = dram("x2t", [NTOK, D], BF16)
    x3f = dram("x3f", [NTOK, D], F32)
    x4t = dram("x4t", [NTOK, D], BF16)
    QT = dram("QT", [NB, NCH, DT, 128, C], BF16)
    KT = dram("KT", [NB, NCH, DT, 128, C], BF16)
    GT = dram("GT", [NB, NCH, DT, 128, C], BF16)
    ROT = dram("ROT", [NB, NCH, DT, 128, C], BF16)
    VT = dram("VT", [NB, NCH, DT, 128, C], BF16)

    sc = Sched(nc)
    A = nc.alloc_sbuf_tensor

    def _trmm(t, src_ap):
        last = None
        for h in range(H):
            last = t.matmul(ps_h[:, h * 128:(h + 1) * 128],
                            src_ap[:, h * 128:(h + 1) * 128], identb_sb[:],
                            start=(h % 4 == 0), stop=(h % 4 == 3))
        return last


    abuf = A("abuf", [128, DT * RA], BF16)
    bbuf = A("bbuf", [128, 4 * D], BF16)
    wrbuf = A("wrbuf", [128, DT * NE], BF16)
    dmask = A("dmask", [128, H * C], BF16)
    abc = A("abc", [128, D], BF16)
    bbcT = A("bbcT", [128, D], BF16)
    identb_sb = A("identb_sb", [128, 128], BF16)
    identf_sb = A("identf_sb", [128, 128], F32)
    onesrow_b_sb = A("onesrow_b_sb", [1, 128], BF16)
    onesrow_f_sb = A("onesrow_f_sb", [1, 128], F32)
    onescol_f_sb = A("onescol_f_sb", [128, 1], F32)
    esel_sb = A("esel_sb", [NE, RA], BF16)
    biases = A("biases", [128, 6 * DT], F32)
    biasrow = A("biasrow", [1, D], BF16)
    brrow = A("brrow", [1, NE], BF16)
    gamrow = A("gamrow", [1, H], F32)
    lngrow = A("lngrow", [1, H], F32)
    lngcol16 = A("lngcol16", [H, 1], F32)
    lncol = A("lncol", [128, H], F32)
    gccol = A("gccol", [128, H], F32)
    gcbc = A("gcbc", [128, D], F32)
    stats = A("stats", [128, 256], F32)
    cvals_sb = A("cvals_sb", [128, 8], F32)
    CVALS = [EPS, -0.5, float(C), FSCALE, 1.0 / D, 2.0]
    for ci, cv in enumerate(CVALS):
        nc.const_aps.register(F32, cv, cvals_sb[:, ci:ci + 1])

    GL = 1024
    GLH = 512
    NG = NTOK // GL
    xTg = A("xTg", [128, DT * GL], BF16)
    dtmp = A("dtmp", [128, 128], F32)
    expnT_sb = A("expnT_sb", [128, 128], F32)
    ones128f_sb = A("ones128f_sb", [128, 128], F32)
    WSTG = [A(f"WSTG{i}", [128, 2 * D], BF16) for i in range(2)]
    wtile = [A(f"wtile{i}", [128, DT * 128], BF16) for i in range(2)]
    qkstage = [A(f"qkst{i}", [128, 1024], BF16) for i in range(2)]
    wtile3 = [wtile[0][:], wtile[1][:]]
    xbuf = [A(f"xbuf{i}", [128, D], F32) for i in range(2)]
    sjunk = A("sjunk", [128, D], BF16)
    xtout = [A(f"xtout{i}", [128, D], BF16) for i in range(2)]
    hbuf = [A(f"hbuf{i}", [128, 4 * 512], BF16) for i in range(2)]
    mexp = [A(f"mexp{i}", [128, 4 * 512], BF16) for i in range(2)]
    mwT = [A(f"mwT{i}", [NE, 512], BF16) for i in range(2)]
    mwtile = [A(f"mwtile{i}", [128, NE], F32) for i in range(4)]
    etmp = [A(f"etmp{i}", [128, NE], F32) for i in range(4)]
    mtmp = [A(f"mtmp{i}", [128, NE], F32) for i in range(4)]
    xres = [A(f"xres{i}", [128, D], F32) for i in range(2)]
    oub = [A(f"oub{i}", [128, D], F32) for i in range(2)]
    hrep_sb = oub[0][0:16, 0:D]
    lngexp = oub[1][0:1, 0:D]
    r1row = xres[0][0:1, 0:D]
    tpow_sb = xres[1][0:1, 0:D]

    P = nc.alloc_psum_tensor
    ps_h = P("ps_h", [128, 4 * 512], F32)
    ps_r45 = P("ps_r45", [128, 1024], F32)
    ps_sel = P("ps_sel", [128, 1024], F32)

    for dst, src in [(identb_sb[:], identb), (identf_sb[:], identf),
                     (onesrow_b_sb[:], onesrow_b), (onesrow_f_sb[:], onesrow_f),
                     (onescol_f_sb[:], onescol_f), (ones128f_sb[:], ones128f),
                     (expnT_sb[:], expnT), (esel_sb[:], Esel),
                     (gamrow[:], gamma.rearrange("(o h) -> o h", o=1)),
                     (cvals_sb[:], cvals),
                     (hrep_sb, HREP), (tpow_sb, tpow)]:
        sc.dma("sync", "c0", dst, src)
    for i, b in enumerate([bq, bk, bg, gnw, gnb, bv]):
        sc.dma("sync", "c0", biases[:, i * DT:(i + 1) * DT],
               b.rearrange("(t p) -> p t", p=128), allow_slow_non_contiguous=True)
    sc.dma("sync", "c0", lngcol16[:], gamma.rearrange("(h o) -> h o", o=1),
           allow_slow_non_contiguous=True)
    cl = sc.at("c0")

    borow_tok = sc.dma("gpsimd", "bo0", biasrow[:], bo.rearrange("(o d) -> o d", o=1))
    brrow_tok = sc.dma("gpsimd", "br0", brrow[:], fbr)
    abT = sc.dma("gpsimd", "prep", abuf[:], fAT)
    bbT = sc.dma("gpsimd", "prep", bbuf[:], fBT, waits=[abT])
    prep_tok = sc.dma("gpsimd", "prep", wrbuf[:], fWrT, waits=[bbT])

    r2f = lambda ap, p=128: ap.rearrange("(a p) d -> p a d", p=p)
    stg3 = [WSTG[i].rearrange("p (a d) -> p a d", d=D) for i in range(2)]
    prev_st = [None, None]
    wprep = {}
    for w, wB, nm in [(wq, wqB, "q"), (wk, wkB, "k"), (wv, wvB, "v"),
                      (wg, wgB, "g"), (wo, woB, "o")]:
        wr3 = r2f(w)
        wB3 = r2f(wB)
        for qtr in range(8):
            pp = qtr % 2
            ld = sc.dma("gpsimd", f"wc{pp}", stg3[pp][:, 0:2, :],
                        wr3[:, qtr * 2:(qtr + 1) * 2, :],
                        waits=[prev_st[pp]] if prev_st[pp] else [])
            prev_st[pp] = sc.dma("gpsimd", f"wc{pp}", wB3[:, qtr * 2:(qtr + 1) * 2, :],
                                 stg3[pp][:, 0:2, :], waits=[ld])
        wprep[nm] = (prev_st[0], prev_st[1])

    def emit_prep_tables():
        s1 = sc.step("scalar", lambda s: s.activation(lngrow[:], gamrow[:], AF.Ln),
                     waits=[cl], inc="Sc")
        s2 = sc.step("scalar", lambda s: s.activation(lngcol16[:], lngcol16[:], AF.Ln),
                     waits=[cl], inc="Sc")
        m1 = sc.step("tensor", lambda t: t.matmul(ps_h[:, 0:H], onesrow_f_sb[:],
                                                  lngrow[:], start=True, stop=True),
                     waits=[s1], inc="T")
        v1 = sc.step("vector", lambda v: v.tensor_copy(lncol[:], ps_h[:, 0:H]),
                     waits=[m1], inc="V")
        prev_ev = None
        for j in range(4):
            def fmm(t, j=j):
                return t.matmul(ps_r45[0:1, 0:512], lngcol16[:],
                                hrep_sb[:, j * 512:(j + 1) * 512], start=True, stop=True)
            mmj = sc.step("tensor", fmm, waits=[s2, cl] + ([prev_ev] if prev_ev else []),
                          inc="T")
            def fev2(v, j=j):
                return v.tensor_copy(lngexp[:, j * 512:(j + 1) * 512], ps_r45[0:1, 0:512])
            prev_ev = sc.step("vector", fev2, waits=[mmj], inc="V")
        lgdone = prev_ev
        vr1 = sc.step("vector", lambda v: v.tensor_tensor(r1row, lngexp, tpow_sb,
                                                          ALU.mult),
                      waits=[lgdone], inc="V")
        sr1 = sc.step("scalar", lambda s: s.activation(r1row, r1row, AF.Exp),
                      waits=[vr1], inc="Sc")
        sgc = sc.step("scalar", lambda s: s.activation(gccol[:], lncol[:], AF.Exp,
                                                       scale=float(C)),
                      waits=[v1], inc="Sc")
        abc_mm_last = None
        for dst, row, dep in [(abc, r1row, sr1)]:
            for j in range(4):
                def fmm(t, row=row, j=j):
                    return t.matmul(ps_r45[:, 512:1024], onesrow_f_sb[:],
                                    row[:, j * 512:(j + 1) * 512], start=True, stop=True)
                abc_mm_last = sc.step("tensor", fmm, waits=[dep, sc.at("V")], inc="T")
                def fev3(v, dst=dst, j=j):
                    return v.tensor_copy(dst[:, j * 512:(j + 1) * 512], ps_r45[:, 512:1024])
                sc.step("vector", fev3, waits=[abc_mm_last], inc="V")
        for h in range(H):
            sc.step("vector", lambda v, h=h: v.tensor_scalar(
                gcbc[:, h * 128:(h + 1) * 128], ones128f_sb[:], gccol[:, h:h + 1],
                None, ALU.mult), waits=[sgc], inc="V")
        abc_done = sc.at("V")
        prev_bb = None
        for h in range(H):
            vrcp = sc.step("vector", lambda v, h=h: v.reciprocal(
                dtmp[:], abc[:, h * 128:(h + 1) * 128]),
                waits=[abc_done] if h == 0 else [prev_bb], inc="V")
            prev_bb = sc.step("vector", lambda v, h=h: v.tensor_scalar(
                bbcT[:, h * 128:(h + 1) * 128], dtmp[:],
                gccol[:, h:h + 1], None, ALU.mult), waits=[vrcp, sgc], inc="V")
        for h in range(H):
            va = sc.step("vector", lambda v, h=h: v.tensor_scalar(
                dtmp[:], expnT_sb[:], lncol[:, h:h + 1], None, ALU.mult),
                waits=[v1, cl, sc.at("Sc"), prev_bb], inc="V")
            sc.step("scalar", lambda s, h=h: s.activation(
                dmask[:, h * 128:(h + 1) * 128], dtmp[:], AF.Exp), waits=[va], inc="Sc")
        return [abc_mm_last, sr1]

    def emit_ln_pass(src_ap, dst_ap, scol0, first_waits=(), tile_waits=None):
        stores = []
        applies = []
        for i in range(NTT):
            bi = i % 2
            w = list(first_waits) if i == 0 else []
            if i >= 2:
                w.append(applies[i - 2])
            if tile_waits is not None:
                w.append(tile_waits[i])
            ld = sc.dma("sync", f"lx{bi}", xbuf[bi][:],
                        src_ap[i * 128:(i + 1) * 128, :], waits=w)
            c0 = scol0 + (i % 4) * 8
            sq = sc.step("scalar", lambda s, bi=bi, c0=c0: s.activation(
                sjunk[:], xbuf[bi][:], AF.Square, accum_out=stats[:, c0 + 1:c0 + 2]),
                waits=[ld] + ([applies[i - 4]] if i >= 4 else []), inc="Sc")
            rv = sc.step("vector", lambda v, bi=bi, c0=c0: v.reduce_sum(
                stats[:, c0:c0 + 1], xbuf[bi][:], axis=AX),
                waits=[ld] + ([applies[i - 4]] if i >= 4 else []), inc="V")
            vmu = sc.step("vector", lambda v, c0=c0: v.tensor_scalar(
                stats[:, c0 + 2:c0 + 3], stats[:, c0:c0 + 1], 1.0 / D, None, ALU.mult),
                waits=[rv], inc="V")
            vm2_ = sc.step("vector", lambda v, c0=c0: v.tensor_tensor(
                stats[:, c0 + 3:c0 + 4], stats[:, c0 + 2:c0 + 3],
                stats[:, c0 + 2:c0 + 3], ALU.mult), waits=[vmu], inc="V")
            vva = sc.step("vector", lambda v, c0=c0: v.scalar_tensor_tensor(
                stats[:, c0 + 4:c0 + 5], stats[:, c0 + 1:c0 + 2], 1.0 / D,
                stats[:, c0 + 3:c0 + 4], ALU.mult, ALU.subtract),
                waits=[vm2_, sq], inc="V")
            sl = sc.step("scalar", lambda s, c0=c0: s.activation(
                stats[:, c0 + 5:c0 + 6], stats[:, c0 + 4:c0 + 5], AF.Ln, bias=EPS),
                waits=[vva], inc="Sc")
            se = sc.step("scalar", lambda s, c0=c0: s.activation(
                stats[:, c0 + 6:c0 + 7], stats[:, c0 + 5:c0 + 6], AF.Exp, scale=-0.5),
                waits=[sl], inc="Sc")
            ob = i % 2
            ap_w = [se]
            if i >= 2:
                ap_w.append(stores[i - 2])
            vap = sc.step("vector", lambda v, bi=bi, ob=ob, c0=c0: v.tensor_scalar(
                xtout[ob][:], xbuf[bi][:], stats[:, c0 + 2:c0 + 3],
                stats[:, c0 + 6:c0 + 7], ALU.subtract, ALU.mult),
                waits=ap_w, inc="V")
            applies.append(vap)
            stores.append(sc.dma("scalar", f"sx{ob}", dst_ap[i * 128:(i + 1) * 128, :],
                                 xtout[ob][:], waits=[vap]))
        return stores

    def emit_ffn_pass(xt_ap, res_ap, out_ap, xt_stores, res_tile_waits=None,
                      ln_dst=None, scol0=64, entry_waits=()):
        out_stores = [None] * NTT
        ln_stores = [None] * NTT if ln_dst is not None else None
        ln_applies = [None] * NTT
        qsc0 = [sc.at("sx0"), sc.at("sx1")]
        hev_last = sc.at("V")
        sel_last_by_chunk = []
        r45_free = [sc.at("V"), sc.at("V")]
        sel_free = [sc.at("V"), sc.at("V")]
        mw_ev_prev = [sc.at("V")] * 4
        me_mm_by_cb = [None, None]
        res_applies = []
        hev_by_chunk = []
        for tg in range(NG):
            tla = tlb = None
            for kt in range(DT):
                w = list(entry_waits)
                if xt_stores is not None:
                    w.append(xt_stores[tg * 8 + 7])
                    w.append(xt_stores[tg * 8 + 6])
                if tg > 0:
                    w.append(sc.at("T"))
                q, clsn = ("sync", "tl") if kt < 8 else ("sync", "tlb")
                t_ = sc.dmat(q, clsn, xTg[:, kt * GL:(kt + 1) * GL],
                             xt_ap[tg * GL:(tg + 1) * GL, kt * 128:(kt + 1) * 128],
                             waits=w)
                if kt < 8:
                    tla = t_
                else:
                    tlb = t_
            for ch in range(GL // 512):
                gch = tg * (GL // 512) + ch
                cb = gch % 2
                t0 = tg * GL + ch * 512
                w = [tla, tlb, r45_free[0]] + list(entry_waits)
                def frt(t, ch=ch):
                    last = None
                    for tt in range(4):
                        L = ps_r45[:, tt * 128:tt * 128 + NE]
                        t.matmul(L, onesrow_b_sb[:], brrow[:],
                                 start=(tt == 0), stop=False)
                        for kt in range(DT):
                            last = t.matmul(
                                L, xTg[:, kt * GL + ch * 512 + tt * 128:
                                       kt * GL + ch * 512 + (tt + 1) * 128],
                                wrbuf[:, kt * NE:(kt + 1) * NE],
                                start=False, stop=(tt == 3 and kt == DT - 1))
                    return last
                rmm = sc.step("tensor", frt, waits=w, inc="T")
                w = [tla, tlb, hev_last] + list(entry_waits)
                def fh(t, ch=ch):
                    last = None
                    for mt in range(4):
                        for kt in range(DT):
                            last = t.matmul(
                                ps_h[:, mt * 512:(mt + 1) * 512],
                                abuf[:, kt * RA + mt * 128:kt * RA + (mt + 1) * 128],
                                xTg[:, kt * GL + ch * 512:kt * GL + (ch + 1) * 512],
                                start=(kt == 0), stop=(kt == DT - 1))
                    return last
                hmm = sc.step("tensor", fh, waits=w, inc="T")
                vmw_l, vmk_l, sex_l = [], [], []
                for tt in range(4):
                    L = ps_r45[:, tt * 128:tt * 128 + NE]
                    c0 = 96 + ((gch % 2) * 4 + tt) * 4
                    vmx = sc.step("vector", lambda v, L=L, c0=c0: v.reduce_max(
                        stats[:, c0:c0 + 1], L, axis=AX, negate=True),
                        waits=[rmm], inc="V")
                    sex = sc.step("scalar", lambda s, L=L, tt=tt, c0=c0: s.activation(
                        etmp[tt][:], L, AF.Exp, bias=stats[:, c0:c0 + 1]),
                        waits=[vmx], inc="Sc")
                    vsm = sc.step("vector", lambda v, tt=tt, c0=c0: v.reduce_sum(
                        stats[:, c0 + 1:c0 + 2], etmp[tt][:], axis=AX),
                        waits=[sex], inc="V")
                    vrc = sc.step("vector", lambda v, c0=c0: v.reciprocal(
                        stats[:, c0 + 2:c0 + 3], stats[:, c0 + 1:c0 + 2]),
                        waits=[vsm], inc="V")
                    vsc = sc.step("vector", lambda v, c0=c0: v.tensor_scalar(
                        stats[:, c0 + 3:c0 + 4], stats[:, c0 + 2:c0 + 3],
                        FSCALE, None, ALU.mult), waits=[vrc], inc="V")
                    vmk = sc.step("vector", lambda v, L=L, tt=tt, c0=c0: v.tensor_scalar(
                        mtmp[tt][:], L, stats[:, c0:c0 + 1], 0.0, ALU.add,
                        ALU.is_equal), waits=[vsc], inc="V")
                    vmw = sc.step("vector", lambda v, tt=tt, c0=c0: v.tensor_scalar(
                        mwtile[tt][:], mtmp[tt][:], stats[:, c0 + 3:c0 + 4],
                        None, ALU.mult), waits=[vmk, mw_ev_prev[tt]], inc="V")
                    vmw_l.append(vmw)
                    vmk_l.append(vmk)
                    sex_l.append(sex)
                w2 = [vmw_l[3], sel_free[0]]
                def ftmw(t):
                    last = None
                    for tt in range(4):
                        last = t.matmul(ps_sel[0:8, tt * 128:(tt + 1) * 128],
                                        mwtile[tt][:], identf_sb[:],
                                        start=(tt == 0), stop=(tt == 3))
                    return last
                tmwg = sc.step("tensor", ftmw, waits=w2, inc="T")
                for tt in range(4):
                    mw_ev_prev[tt] = tmwg
                evw = [tmwg]
                if me_mm_by_cb[cb] is not None:
                    evw.append(me_mm_by_cb[cb])
                ev = sc.step("scalar", lambda s, cb=cb: s.copy(
                    mwT[cb][:], ps_sel[0:8, 0:512]), waits=evw, inc="Sc")
                me_evs = []
                me_banks = [ps_r45[:, 512:1024], ps_r45[:, 0:512]]
                for rp in range(4):
                    bk_ = rp % 2
                    w = [ev]
                    if bk_ == 1:
                        w += [vmk_l[3], sex_l[3]]
                    else:
                        w.append(r45_free[1])
                    if rp >= 2:
                        w.append(me_evs[rp - 2])
                    def fme(t, rp=rp, bk_=bk_, cb=cb):
                        return t.matmul(me_banks[bk_],
                                        esel_sb[:, rp * 128:(rp + 1) * 128],
                                        mwT[cb][:], start=True, stop=True)
                    mm = sc.step("tensor", fme, waits=w, inc="T")
                    if rp == 3:
                        me_mm_by_cb[cb] = mm
                    me_evs.append(sc.step("scalar", lambda s, rp=rp, bk_=bk_, cb=cb:
                                          s.copy(
                                              mexp[cb][:, rp * 512:(rp + 1) * 512],
                                              me_banks[bk_]),
                                          waits=[mm], inc="Sc"))
                r45_free = [me_evs[3], me_evs[2]]
                hev = None
                for mt in range(4):
                    w = [hmm, me_evs[mt]]
                    if len(sel_last_by_chunk) >= 2:
                        w.append(sel_last_by_chunk[-2])
                    hev = sc.step("vector", lambda v, mt=mt, cb=cb: v.tensor_tensor(
                        hbuf[cb][:, mt * 512:(mt + 1) * 512],
                        ps_h[:, mt * 512:(mt + 1) * 512],
                        mexp[cb][:, mt * 512:(mt + 1) * 512], ALU.mult),
                        waits=w, inc="V")
                hev_last = hev
                hev_by_chunk.append(hev)
                for tt in range(4):
                    i = (t0 // 128) + tt
                    rb = i % 2
                    w = []
                    if len(res_applies) >= 2:
                        w.append(res_applies[-2])
                    else:
                        w.extend(entry_waits)
                    if res_tile_waits is not None:
                        w.append(res_tile_waits[i])
                    rld = sc.dma("sync", f"rx{rb}", xres[rb][:],
                                 res_ap[i * 128:(i + 1) * 128, :], waits=w)
                    ob = i % 2
                    qevs = []
                    for oq in range(4):
                        bk_ = oq % 2
                        w = [hev, sel_free[bk_]]
                        if oq >= 2:
                            w.append(qevs[oq - 2])
                        def fsel(t, tt=tt, oq=oq, bk_=bk_, cb=cb):
                            last = None
                            for rp in range(4):
                                last = t.matmul(
                                    ps_sel[:, bk_ * 512:(bk_ + 1) * 512],
                                    hbuf[cb][:, rp * 512 + tt * 128:
                                              rp * 512 + (tt + 1) * 128],
                                    bbuf[:, rp * D + oq * 512:rp * D + (oq + 1) * 512],
                                    start=(rp == 0), stop=(rp == 3))
                            return last
                        smm = sc.step("tensor", fsel, waits=w, inc="T")
                        wq_ = [smm, rld]
                        if i >= 2 and out_stores[i - 2]:
                            wq_.append(out_stores[i - 2])
                        elif i < 2:
                            wq_.extend(entry_waits)
                        if ln_dst is not None and i >= 2 and ln_applies[i - 2]:
                            wq_.append(ln_applies[i - 2])
                        qev = sc.step("vector", lambda v, oq=oq, bk_=bk_, rb=rb, ob=ob:
                                      v.tensor_tensor(
                                          oub[ob][:, oq * 512:(oq + 1) * 512],
                                          ps_sel[:, bk_ * 512:(bk_ + 1) * 512],
                                          xres[rb][:, oq * 512:(oq + 1) * 512],
                                          ALU.add),
                                      waits=wq_, inc="V")
                        qevs.append(qev)
                        sel_free[bk_] = qev
                    sel_last_by_chunk.append(smm) if tt == 3 else None
                    res_applies.append(qevs[-1])
                    ost = sc.dma("scalar", f"so{ob}", out_ap[i * 128:(i + 1) * 128, :],
                                 oub[ob][:], waits=[qevs[-1]])
                    out_stores[i] = ost
                    if ln_dst is not None:
                        c0 = scol0 + (i % 4) * 8
                        sq = sc.step("scalar", lambda s, ob=ob, c0=c0: s.activation(
                            sjunk[:], oub[ob][:], AF.Square,
                            accum_out=stats[:, c0 + 1:c0 + 2]),
                            waits=[qevs[-1]], inc="Sc")
                        rv = sc.step("vector", lambda v, ob=ob, c0=c0: v.reduce_sum(
                            stats[:, c0:c0 + 1], oub[ob][:], axis=AX),
                            waits=[qevs[-1]], inc="V")
                        vmu = sc.step("vector", lambda v, c0=c0: v.tensor_scalar(
                            stats[:, c0 + 2:c0 + 3], stats[:, c0:c0 + 1], 1.0 / D,
                            None, ALU.mult), waits=[rv], inc="V")
                        vm2_ = sc.step("vector", lambda v, c0=c0: v.tensor_tensor(
                            stats[:, c0 + 3:c0 + 4], stats[:, c0 + 2:c0 + 3],
                            stats[:, c0 + 2:c0 + 3], ALU.mult), waits=[vmu], inc="V")
                        vva = sc.step("vector", lambda v, c0=c0: v.scalar_tensor_tensor(
                            stats[:, c0 + 4:c0 + 5], stats[:, c0 + 1:c0 + 2], 1.0 / D,
                            stats[:, c0 + 3:c0 + 4], ALU.mult, ALU.subtract),
                            waits=[vm2_, sq], inc="V")
                        sl = sc.step("scalar", lambda s, c0=c0: s.activation(
                            stats[:, c0 + 5:c0 + 6], stats[:, c0 + 4:c0 + 5],
                            AF.Ln, bias=EPS), waits=[vva], inc="Sc")
                        se = sc.step("scalar", lambda s, c0=c0: s.activation(
                            stats[:, c0 + 6:c0 + 7], stats[:, c0 + 5:c0 + 6],
                            AF.Exp, scale=-0.5), waits=[sl], inc="Sc")
                        lw = [se]
                        if i >= 2 and ln_stores[i - 2]:
                            lw.append(ln_stores[i - 2])
                        else:
                            lw.extend(qsc0)
                        vap = sc.step("vector", lambda v, ob=ob, c0=c0: v.tensor_scalar(
                            xtout[ob][:], oub[ob][:], stats[:, c0 + 2:c0 + 3],
                            stats[:, c0 + 6:c0 + 7], ALU.subtract, ALU.mult),
                            waits=lw, inc="V")
                        ln_applies[i] = vap
                        ln_stores[i] = sc.dma("scalar", f"sx{ob}",
                                              ln_dst[i * 128:(i + 1) * 128, :],
                                              xtout[ob][:], waits=[vap])
        return out_stores, ln_stores

    xtg_free = emit_prep_tables()
    p1a_stores = emit_ln_pass(xs, x1t, 0, first_waits=[cl])
    setup_done = [sc.at("V"), sc.at("Sc"), sc.at("T"), sc.at("bo0"),
                  sc.at("br0"),
                  sc.at("c0"), sc.at("prep")]
    ffn_entry = [brrow_tok, prep_tok] + xtg_free
    x1_stores, x2t_stores = emit_ffn_pass(x1t, xs, x1f, p1a_stores,
                                          ln_dst=x2t, scol0=64,
                                          entry_waits=ffn_entry)

    proj_final = {}
    ro_store = {}
    if "P2" in stages:
        psh2 = [ps_r45[:], ps_sel[:]]
        w3 = {"q": wqB, "k": wkB, "v": wvB, "g": wgB}
        dst = {"q": QT, "k": KT, "v": VT, "g": GT}
        bcol = {"q": 0, "k": 1, "v": 5, "g": 2}
        p2_V0 = sc.at("V")
        prev_mm = {0: None, 1: None}
        prev_store = {0: None, 1: None}
        prev_ev = [p2_V0, p2_V0]
        cur_tl = [None]

        def emit_p2_tloads(tg):
            guard = [t for t in (prev_mm[0], prev_mm[1]) if t] or [sc.at("T")]
            tl2 = [None, None]
            for kt in range(DT):
                w = [x2t_stores[tg * 8 + 7], x2t_stores[tg * 8 + 6]] + guard
                q, clsn, slot = ("sync", "tl", 0) if kt < 8 else ("sync", "tlb", 1)
                tl2[slot] = sc.dmat(q, clsn, xTg[:, kt * GL:(kt + 1) * GL],
                                    x2t[tg * GL:(tg + 1) * GL,
                                        kt * 128:(kt + 1) * 128],
                                    waits=w)
            cur_tl[0] = tl2

        def emit_p2_m(nm, tg, m):
            tl2 = cur_tl[0]
            b0 = tg // 2
            ch0 = (tg % 2) * 8
            wB3 = w3[nm].rearrange("(a p) d -> p a d", p=128)
            gi = m % 2
            wwt = list(wprep[nm])
            if prev_mm[gi] is not None:
                wwt.append(prev_mm[gi])
            wld = sc.dma("sync", f"wt{gi}",
                         wtile[gi].rearrange("p (a d) -> p a d", d=128),
                         wB3[:, :, m * 128:(m + 1) * 128], waits=wwt)
            hb = m % 2
            def fp2(t, gi=gi, hb=hb):
                last = None
                for kt in range(DT):
                    for n2 in range(2):
                        last = t.matmul(
                            psh2[hb][:, n2 * 512:(n2 + 1) * 512],
                            wtile[gi][:, kt * 128:(kt + 1) * 128],
                            xTg[:, kt * GL + n2 * 512:kt * GL + (n2 + 1) * 512],
                            start=(kt == 0), stop=(kt == DT - 1))
                return last
            mm = sc.step("tensor", fp2, waits=tl2 + [wld, prev_ev[hb]], inc="T")
            prev_mm[gi] = mm
            sp = m % 2
            wev = [mm]
            if prev_store[sp] is not None:
                wev.append(prev_store[sp])
            if nm == "g":
                ev = sc.step("scalar", lambda s, m=m, hb=hb, sp=sp: s.activation(
                    qkstage[sp][:], psh2[hb][:], AF.Silu,
                    bias=biases[:, bcol["g"] * DT + m:bcol["g"] * DT + m + 1]),
                    waits=wev, inc="Sc")
            else:
                ev = sc.step("vector", lambda s, m=m, hb=hb, sp=sp, nm=nm:
                             s.tensor_scalar(
                                 qkstage[sp][:], psh2[hb][:],
                                 biases[:, bcol[nm] * DT + m:bcol[nm] * DT + m + 1],
                                 None, ALU.add), waits=wev, inc="V")
            prev_ev[hb] = ev
            st = sc.dma("scalar", f"sqk{sp}",
                        dst[nm][b0, ch0:ch0 + 8, m].rearrange("c p t -> p c t"),
                        qkstage[sp].rearrange("p (c t) -> p c t", t=C),
                        waits=[ev])
            prev_store[sp] = st
            if m == DT - 1:
                proj_final[(nm, tg)] = (prev_store[0], prev_store[1])

    if "P3" in stages:
        xbufB = [xb.bitcast(BF16) for xb in xbuf]
        xresB = [xr.bitcast(BF16) for xr in xres]
        qch = [xbufB[0][:, 0:D], xbufB[0][:, D:2 * D]]
        kch = [xbufB[1][:, 0:D], xbufB[1][:, D:2 * D]]
        vtch = [xresB[0][:, 0:D], xresB[0][:, D:2 * D]]
        vtok = [xresB[1][:, 0:D], xresB[1][:, D:2 * D]]
        khatT = [hbuf[0][:], hbuf[1][:]]
        khtok = [mexp[0][:], mexp[1][:]]
        smask = [WSTG[0][:, 0:D], WSTG[0][:, D:2 * D]]
        retsb = [WSTG[1][:, 0:D], WSTG[1][:, D:2 * D]]
        qhat = qch
        st_f = [oub[0], oub[1]]
        st_b = [xtout[0], xtout[1]]
        statacc = stats[:, 128:192]

        p3v0 = sc.at("V")
        p3s0 = sc.at("Sc")
        mz = sc.step("vector", lambda v: v.memset(statacc, 0.0), waits=[p3v0],
                     inc="V")
        stz = [sc.step("vector", lambda v, b=b: v.memset(st_f[b][:], 0.0),
                       waits=[p3v0, sc.at("so0"), sc.at("so1")], inc="V")
               for b in range(NB)]
        t3 = dict(ld_gate=[(), ()], sppr=[None, None],
                  psh_free=[sc.at("Sc"), sc.at("V"), p3v0],
                  prev_cast=[None, None], prev_sro=[None, None],
                  prev_stats=None, prev_gsm=None, prev_wst=[(), ()])

        def p3_unit_gen(c, b):
            it = c * 2 + b
            cb = it % 2
            pf = lambda nm_: list(proj_final.get((nm_, b * 2 + c // 8), ()))
            gate_ = list(t3["ld_gate"][cb])
            r3 = lambda ap: ap.rearrange("dt p t -> p dt t")
            s3 = lambda ap: ap.rearrange("p (dt t) -> p dt t", t=C)
            lq = sc.dma("gpsimd", f"q3{cb}", s3(qch[cb]), r3(QT[b, c]),
                        waits=pf("q") + gate_)
            lk = sc.dma("gpsimd", f"k3{cb}", s3(kch[cb]), r3(KT[b, c]),
                        waits=pf("k") + gate_)
            lv = sc.dma("gpsimd", f"v3{cb}", s3(vtch[cb]), r3(VT[b, c]),
                        waits=pf("v") + gate_)
            wg_ = [t3["sppr"][cb]] if t3["sppr"][cb] else []
            if it < 2:
                wg_ = wg_ + [setup_done[0]]
            def fs2(t, cb=cb):
                last = None
                for h in range(H):
                    last = t.matmul(ps_h[:, h * 128:(h + 1) * 128],
                                    kch[cb][:, h * 128:(h + 1) * 128],
                                    qch[cb][:, h * 128:(h + 1) * 128],
                                    start=(h % 4 == 0), stop=(h % 4 == 3))
                return last
            s2mm = sc.step("tensor", fs2, waits=[lq, lk] + t3["psh_free"], inc="T")
            gqh = sc.step("gpsimd", lambda g, cb=cb: g.tensor_tensor(
                qhat[cb], qch[cb], abc[:], ALU.mult), waits=[s2mm] + wg_, inc="G")
            gkh = sc.step("gpsimd", lambda g, cb=cb: g.tensor_tensor(
                khatT[cb], kch[cb], bbcT[:], ALU.mult), waits=[s2mm, lk] + wg_,
                inc="G")
            yield
            wsc = [s2mm]
            if t3["prev_gsm"] is not None:
                wsc.append(t3["prev_gsm"])
            scp = sc.step("scalar", lambda s: s.copy(sjunk[:], ps_h[:]),
                          waits=wsc, inc="Sc")
            wsm = [scp] + ([t3["sppr"][cb]] if t3["sppr"][cb] else [])
            if it < 2:
                wsm += [wprep["o"][0], wprep["o"][1]]
            gsm = sc.step("gpsimd", lambda g, cb=cb: g.tensor_tensor(
                smask[cb], sjunk[:], dmask[:], ALU.mult), waits=wsm, inc="G")
            t3["prev_gsm"] = gsm
            trk = sc.step("tensor", lambda t, cb=cb: _trmm(t, khatT[cb]),
                          waits=[gkh, scp], inc="T")
            ekt = sc.step("vector", lambda v, cb=cb: v.tensor_copy(
                khtok[cb], ps_h[:]),
                waits=[trk] + ([t3["sppr"][cb]] if t3["sppr"][cb] else []), inc="V")
            yield
            trv = sc.step("tensor", lambda t, cb=cb: _trmm(t, vtch[cb]),
                          waits=[lv, ekt], inc="T")
            evt = sc.step("scalar", lambda s, cb=cb: s.copy(vtok[cb], ps_h[:]),
                          waits=[trv] + ([t3["sppr"][cb]] if t3["sppr"][cb] else []),
                          inc="Sc")
            yield
            def fret(t, cb=cb, b=b, c=c):
                last = None
                for h in range(H):
                    last = t.matmul(ps_h[:, h * 128:(h + 1) * 128],
                                    vtok[cb][:, h * 128:(h + 1) * 128],
                                    smask[cb][:, h * 128:(h + 1) * 128],
                                    start=(h % 4 == 0),
                                    stop=(h % 4 == 3 and c == 0))
                    if c > 0:
                        last = t.matmul(ps_h[:, h * 128:(h + 1) * 128],
                                        st_b[b][:, h * 128:(h + 1) * 128],
                                        qhat[cb][:, h * 128:(h + 1) * 128],
                                        start=False, stop=(h % 4 == 3))
                return last
            wret = [evt, gsm, gqh]
            if c > 0:
                wret.append(t3["prev_cast"][b])
            rmm3 = sc.step("tensor", fret, waits=wret, inc="T")
            wer = [rmm3]
            if it < 2:
                wer += [wprep["o"][0], wprep["o"][1]]
            if t3["prev_sro"][cb] is not None:
                wer.append(t3["prev_sro"][cb])
            er1 = sc.step("scalar", lambda s, cb=cb: s.copy(
                retsb[cb][:, 0:1024], ps_h[:, 0:1024]), waits=wer, inc="Sc")
            er2 = sc.step("scalar", lambda s, cb=cb: s.copy(
                retsb[cb][:, 1024:2048], ps_h[:, 1024:2048]), waits=[rmm3],
                inc="Sc")
            sro = sc.dma("gpsimd", f"sro{cb}",
                         ROT[b, c].rearrange("dt p t -> p dt t"),
                         retsb[cb].rearrange("p (dt t) -> p dt t", t=C),
                         waits=[er2, er1] +
                         ([t3["prev_sro"][cb]] if t3["prev_sro"][cb] else []))
            t3["prev_sro"][cb] = sro
            pstat = t3["prev_stats"]
            vst1 = sc.step("vector", lambda v, cb=cb: v.reduce_sum(
                stats[:, 224:240], retsb[cb].rearrange("p (s t) -> p s t", s=16),
                axis=AX), waits=[er1, er2] + ([pstat] if pstat else []), inc="V")
            vst2 = sc.step("vector", lambda v, b=b: v.tensor_tensor(
                statacc[:, b * 16:(b + 1) * 16], statacc[:, b * 16:(b + 1) * 16],
                stats[:, 224:240], ALU.add), waits=[vst1, mz], inc="V")
            def fsq(s, cb=cb):
                last = None
                for h in range(H):
                    last = s.activation(dtmp[:],
                                        retsb[cb][:, h * 128:(h + 1) * 128],
                                        AF.Square,
                                        accum_out=stats[:, 240 + h:241 + h])
                return last
            ssq = sc.step("scalar", fsq, waits=[er1, er2] +
                          ([pstat] if pstat else []), inc="Sc")
            vst4 = sc.step("vector", lambda v, b=b: v.tensor_tensor(
                statacc[:, 32 + b * 16:32 + (b + 1) * 16],
                statacc[:, 32 + b * 16:32 + (b + 1) * 16],
                stats[:, 240:256], ALU.add), waits=[ssq, vst2], inc="V")
            t3["prev_stats"] = vst4
            yield
            def fds(t, cb=cb):
                last = None
                for h in range(H):
                    last = t.matmul(ps_h[:, h * 128:(h + 1) * 128],
                                    khtok[cb][:, h * 128:(h + 1) * 128],
                                    vtok[cb][:, h * 128:(h + 1) * 128],
                                    start=(h % 4 == 0), stop=(h % 4 == 3))
                return last
            dsm = sc.step("tensor", fds, waits=[ekt, evt, er1, er2], inc="T")
            t3["sppr"][cb] = dsm
            wd = [stz[b]] + list(t3["prev_wst"][b])
            if t3["prev_cast"][b] is not None:
                wd.append(t3["prev_cast"][b])
            wst1 = sc.step("vector", lambda v, b=b: v.tensor_tensor(
                st_f[b][:], st_f[b][:], gcbc[:], ALU.mult), waits=wd, inc="V")
            wst2 = sc.step("vector", lambda v, b=b: v.tensor_tensor(
                st_f[b][:, 0:1024], st_f[b][:, 0:1024], ps_h[:, 0:1024], ALU.add),
                waits=[dsm, wst1], inc="V")
            wst3 = sc.step("vector", lambda v, b=b: v.tensor_tensor(
                st_f[b][:, 1024:2048], st_f[b][:, 1024:2048], ps_h[:, 1024:2048],
                ALU.add), waits=[dsm, wst1], inc="V")
            t3["prev_wst"][b] = (wst2, wst3)
            t3["psh_free"] = [wst2, wst3]
            cast = sc.step("scalar", lambda s, b=b: s.copy(st_b[b][:], st_f[b][:]),
                           waits=[wst2, wst3], inc="Sc")
            t3["prev_cast"][b] = cast
            t3["ld_gate"][cb] = (rmm3, gqh, gkh, gsm, dsm)

    if "P2" in stages:
        aorder = [0, 2, 1, 3]
        p3_ready = {}
        for i, tg in enumerate(aorder):
            b_ = tg // 2
            for c_ in range((tg % 2) * 8, (tg % 2) * 8 + 8):
                p3_ready[(c_, b_)] = i + 1
        p3_list = [(c_, b_) for c_ in range(NCH) for b_ in range(NB)]
        p3_state = {"idx": 0, "gen": None}

        def pump_p3(done_sub, credits):
            if "P3" not in stages:
                return
            n = 0
            while n < credits:
                if p3_state["gen"] is None:
                    if (p3_state["idx"] >= len(p3_list) or
                            p3_ready[p3_list[p3_state["idx"]]] > done_sub):
                        return
                    c_, b_ = p3_list[p3_state["idx"]]
                    p3_state["gen"] = p3_unit_gen(c_, b_)
                try:
                    next(p3_state["gen"])
                except StopIteration:
                    p3_state["gen"] = None
                    p3_state["idx"] += 1
                n += 1

        for i, tg in enumerate(aorder):
            emit_p2_tloads(tg)
            pump_p3(i, 4)
            for nm in ["q", "k", "v"]:
                for m in range(DT):
                    emit_p2_m(nm, tg, m)
                    pump_p3(i, 1)
        for i, tg in enumerate(aorder):
            emit_p2_tloads(tg)
            pump_p3(4, 4)
            for m in range(DT):
                emit_p2_m("g", tg, m)
                pump_p3(4, 2)
        pump_p3(4, 400)

    if "P3" in stages:
        gn_done = []
        for b in range(NB):
            def fgn(t, b=b):
                t.matmul(ps_r45[0:1, 0:16], onescol_f_sb[:],
                         statacc[:, b * 16:(b + 1) * 16], start=True, stop=False)
                return t.matmul(ps_r45[0:1, 16:32], onescol_f_sb[:],
                                statacc[:, 32 + b * 16:32 + (b + 1) * 16],
                                start=False, stop=True)
            gmm = sc.step("tensor", fgn,
                          waits=[t3["prev_stats"], prev_ev[0], prev_ev[1],
                                 t3["psh_free"][0], t3["psh_free"][1]] + gn_done,
                          inc="T")
            NSD = float(S * 128)
            g1 = sc.step("vector", lambda v: v.tensor_scalar(
                stats[0:1, 96:112], ps_r45[0:1, 0:16], 1.0 / NSD, None, ALU.mult),
                waits=[gmm], inc="V")
            g2 = sc.step("vector", lambda v: v.tensor_scalar(
                stats[0:1, 112:128], ps_r45[0:1, 16:32], 1.0 / NSD, None,
                ALU.mult), waits=[g1], inc="V")
            g3 = sc.step("vector", lambda v: v.tensor_tensor(
                stats[0:1, 240:256], stats[0:1, 96:112], stats[0:1, 96:112],
                ALU.mult), waits=[g2], inc="V")
            g4 = sc.step("vector", lambda v: v.tensor_tensor(
                stats[0:1, 112:128], stats[0:1, 112:128], stats[0:1, 240:256],
                ALU.subtract), waits=[g3], inc="V")
            g5 = sc.step("scalar", lambda s: s.activation(
                stats[0:1, 240:256], stats[0:1, 112:128], AF.Ln, bias=EPS),
                waits=[g4], inc="Sc")
            g6 = sc.step("scalar", lambda s: s.activation(
                stats[0:1, 112:128], stats[0:1, 240:256], AF.Exp, scale=-0.5),
                waits=[g5], inc="Sc")
            def fbc(t, b=b):
                t.matmul(ps_r45[:, 512:528], onesrow_f_sb[:],
                         stats[0:1, 112:128], start=True, stop=False)
                return t.matmul(ps_r45[:, 528:544], onesrow_f_sb[:],
                                stats[0:1, 96:112], start=False, stop=True)
            bmm = sc.step("tensor", fbc, waits=[g6], inc="T")
            g8 = sc.step("vector", lambda v, b=b: v.tensor_tensor(
                stats[:, 192 + b * 16:192 + (b + 1) * 16], biases[:, 3 * DT:4 * DT],
                ps_r45[:, 512:528], ALU.mult), waits=[bmm], inc="V")
            g9 = sc.step("vector", lambda v, b=b: v.tensor_tensor(
                stats[:, 240:256], stats[:, 192 + b * 16:192 + (b + 1) * 16],
                ps_r45[:, 528:544], ALU.mult), waits=[g8], inc="V")
            g10 = sc.step("vector", lambda v, b=b: v.tensor_tensor(
                stats[:, 224 + b * 16:224 + (b + 1) * 16], biases[:, 4 * DT:5 * DT],
                stats[:, 240:256], ALU.subtract), waits=[g9], inc="V")
            gn_done.append(g10)

    x3_stores = [None] * NTT
    x4t_stores = [None] * NTT
    if "P4" in stages:
        wo_home = ([xTg[:, kt * D:(kt + 1) * D] for kt in range(8)] +
                   [wtile[0][:], wtile[1][:], WSTG[1][:, 0:D], WSTG[1][:, D:2 * D],
                    None,
                    WSTG[0][:, D:2 * D], mexp[0][:], mexp[1][:]])

        def wo_ap(kt, oq):
            if kt == 12:
                return qkstage[oq // 2][:, (oq % 2) * 512:(oq % 2) * 512 + 512]
            return wo_home[kt][:, oq * 512:(oq + 1) * 512]

        wo_gate = ([sc.at("T"), sc.at("Sc"), sc.at("V"), wprep["o"][0],
                    wprep["o"][1], t3["prev_sro"][0], t3["prev_sro"][1]] +
                   list(proj_final.get(("g", 3), ())))
        wo_lds = []
        for kt in range(DT):
            if kt == 12:
                wo_lds.append(sc.dma("sync", "wo5", qkstage[0][:],
                                     woB[kt * 128:(kt + 1) * 128, 0:1024],
                                     waits=wo_gate))
                wo_lds.append(sc.dma("sync", "wo5", qkstage[1][:],
                                     woB[kt * 128:(kt + 1) * 128, 1024:2048],
                                     waits=wo_gate))
            else:
                wo_lds.append(sc.dma("sync", "wo5", wo_home[kt],
                                     woB[kt * 128:(kt + 1) * 128, :],
                                     waits=wo_gate))
        roch = [xbufB[0][:, 0:D], xbufB[0][:, D:2 * D]]
        gch2 = [xbufB[1][:, 0:D], xbufB[1][:, D:2 * D]]
        rogate = [hbuf[0][:], hbuf[1][:]]
        x1res = [xres[0], xres[1]]
        outb = [oub[0], oub[1]]
        p4v0 = sc.at("V")
        p4t0 = sc.at("T")
        p4s0 = sc.at("Sc")
        p4_psum = [(ps_r45[:], ps_sel[:]), (ps_h[:, 0:1024], ps_h[:, 1024:2048])]
        prev_ld4 = [None, None]
        prev_x1 = [None, None]
        prev_ps_ev = [[p4v0, p4v0], [p4v0, p4v0]]
        prev_rg = [None, None]
        prev_oub_st = [None, None]
        prev_ln_ap = [None, None]
        prev_ln_st = [None, None]
        p4_iters = [(b, c) for b in range(NB) for c in range(NCH)]
        preps = {}
        lx1s = {}

        def p4_prep(i):
            b, c = p4_iters[i]
            cb = i % 2
            wro = [t3["prev_sro"][0], t3["prev_sro"][1], gn_done[b]]
            if c == 0:
                wro.append(sc.at("G"))
            if prev_ld4[cb] is not None:
                wro.append(prev_ld4[cb])
            r3 = lambda ap: ap.rearrange("dt p t -> p dt t")
            s3 = lambda ap: ap.rearrange("p (dt t) -> p dt t", t=C)
            lro = sc.dma("gpsimd", f"ro4{cb}", s3(roch[cb]), r3(ROT[b, c]),
                         waits=wro)
            lg = sc.dma("gpsimd", f"g4{cb}", s3(gch2[cb]), r3(GT[b, c]),
                        waits=list(proj_final.get(("g", b * 2 + c // 8), ())) +
                              ([prev_ld4[cb]] if prev_ld4[cb] else []))
            lx1s[i] = sc.dma("sync", f"rx4{cb}", x1res[cb][:],
                             x1f[i * 128:(i + 1) * 128, :],
                             waits=[prev_x1[cb]] if prev_x1[cb] else [])
            tga = None
            for dt_ in range(DT):
                wga = [lro, gn_done[b]]
                if dt_ == 0 and prev_rg[cb] is not None:
                    wga.append(prev_rg[cb])
                tga = sc.step("vector", lambda v, cb=cb, b=b, dt_=dt_: v.tensor_scalar(
                    sjunk[:, dt_ * 128:(dt_ + 1) * 128],
                    roch[cb][:, dt_ * 128:(dt_ + 1) * 128],
                    stats[:, 192 + b * 16 + dt_:192 + b * 16 + dt_ + 1],
                    stats[:, 224 + b * 16 + dt_:224 + b * 16 + dt_ + 1],
                    ALU.mult, ALU.add), waits=wga, inc="V")
            vrg = sc.step("vector", lambda v, cb=cb: v.tensor_tensor(
                rogate[cb], sjunk[:], gch2[cb], ALU.mult), waits=[tga, lg],
                inc="V")
            prev_ld4[cb] = vrg
            preps[i] = vrg

        p4_prep(0)
        for b in range(NB):
            for c in range(NCH):
                i = b * NCH + c
                cb = i % 2
                if i + 1 < len(p4_iters):
                    p4_prep(i + 1)
                vrg = preps[i]
                lx1 = lx1s[i]
                pp = i % 2
                psA, psB2 = p4_psum[pp]
                def fop(t, cb=cb, pp=pp):
                    psA, psB2 = p4_psum[pp]
                    last = None
                    for oq in range(4):
                        tgt = psA if oq < 2 else psB2
                        col = (oq % 2) * 512
                        t.matmul(tgt[:, col:col + 512], onesrow_b_sb[:],
                                 biasrow[:, oq * 512:(oq + 1) * 512],
                                 start=True, stop=False)
                        for kt in range(DT):
                            last = t.matmul(
                                tgt[:, col:col + 512],
                                rogate[cb][:, kt * 128:(kt + 1) * 128],
                                wo_ap(kt, oq),
                                start=False, stop=(kt == DT - 1))
                    return last
                wop = [vrg, borow_tok] + prev_ps_ev[pp]
                if i == 0:
                    wop += [p4t0] + ([wo_lds[-1]] if wo_lds else [])
                opm = sc.step("tensor", fop, waits=wop, inc="T")
                prev_rg[cb] = opm
                ob4 = cb
                wev4 = [opm, lx1]
                if prev_oub_st[ob4] is not None:
                    wev4.append(prev_oub_st[ob4])
                if prev_ln_ap[ob4] is not None:
                    wev4.append(prev_ln_ap[ob4])
                e1 = sc.step("vector", lambda v, cb=cb, pp=pp: v.tensor_tensor(
                    outb[cb][:, 0:1024], p4_psum[pp][0], x1res[cb][:, 0:1024],
                    ALU.add), waits=wev4, inc="V")
                e2 = sc.step("vector", lambda v, cb=cb, pp=pp: v.tensor_tensor(
                    outb[cb][:, 1024:2048], p4_psum[pp][1], x1res[cb][:, 1024:2048],
                    ALU.add), waits=[e1], inc="V")
                prev_ps_ev[pp] = [e1, e2]
                prev_x1[cb] = e2
                st4 = sc.dma("scalar", f"so4{cb}", x3f[i * 128:(i + 1) * 128, :],
                             outb[cb][:], waits=[e2])
                x3_stores[i] = st4
                prev_oub_st[ob4] = st4
                c0 = 64 + (i % 4) * 8
                sq = sc.step("scalar", lambda s, cb=cb, c0=c0: s.activation(
                    WSTG[0][:, 0:D], outb[cb][:], AF.Square,
                    accum_out=stats[:, c0 + 1:c0 + 2]), waits=[e2, vrg], inc="Sc")
                rv = sc.step("vector", lambda v, cb=cb, c0=c0: v.reduce_sum(
                    stats[:, c0:c0 + 1], outb[cb][:], axis=AX), waits=[e2], inc="V")
                vmu = sc.step("vector", lambda v, c0=c0: v.tensor_scalar(
                    stats[:, c0 + 2:c0 + 3], stats[:, c0:c0 + 1], 1.0 / D, None,
                    ALU.mult), waits=[rv], inc="V")
                vm2_ = sc.step("vector", lambda v, c0=c0: v.tensor_tensor(
                    stats[:, c0 + 3:c0 + 4], stats[:, c0 + 2:c0 + 3],
                    stats[:, c0 + 2:c0 + 3], ALU.mult), waits=[vmu], inc="V")
                vva = sc.step("vector", lambda v, c0=c0: v.scalar_tensor_tensor(
                    stats[:, c0 + 4:c0 + 5], stats[:, c0 + 1:c0 + 2], 1.0 / D,
                    stats[:, c0 + 3:c0 + 4], ALU.mult, ALU.subtract),
                    waits=[vm2_, sq], inc="V")
                sl = sc.step("scalar", lambda s, c0=c0: s.activation(
                    stats[:, c0 + 5:c0 + 6], stats[:, c0 + 4:c0 + 5], AF.Ln,
                    bias=EPS), waits=[vva], inc="Sc")
                se = sc.step("scalar", lambda s, c0=c0: s.activation(
                    stats[:, c0 + 6:c0 + 7], stats[:, c0 + 5:c0 + 6], AF.Exp,
                    scale=-0.5), waits=[sl], inc="Sc")
                lw = [se]
                if prev_ln_st[cb] is not None:
                    lw.append(prev_ln_st[cb])
                else:
                    lw += [sc.at("sx0"), sc.at("sx1")]
                vap = sc.step("vector", lambda v, cb=cb, c0=c0: v.tensor_scalar(
                    xtout[cb][:], outb[cb][:], stats[:, c0 + 2:c0 + 3],
                    stats[:, c0 + 6:c0 + 7], ALU.subtract, ALU.mult),
                    waits=lw, inc="V")
                prev_ln_ap[cb] = vap
                st5 = sc.dma("scalar", f"sx4{cb}", x4t[i * 128:(i + 1) * 128, :],
                             xtout[cb][:], waits=[vap])
                x4t_stores[i] = st5
                prev_ln_st[cb] = st5
                ld_sq_guard = sq

    if "P5" in stages:
        p5_entry = [brrow_tok, prep_tok, sc.at("T"), sc.at("V"), sc.at("Sc"),
                    sc.at("G"), x3_stores[NTT - 1], x3_stores[NTT - 2]]
        y_stores, _ = emit_ffn_pass(x4t, x3f, y, x4t_stores,
                                    res_tile_waits=x3_stores, ln_dst=None,
                                    scol0=64, entry_waits=p5_entry)
        fin = [sc.at("so0"), sc.at("so1")]
    else:
        fin = []

    drain = [sc.at(s) for s in list(sc.cnt) if s not in ("T", "V", "Sc", "G")]
    sc.step("sync", lambda e: e.nop(), waits=drain + fin)

    sc.emit_and_close()
    return nc


def _consts():
    t = np.arange(128, dtype=np.float32)
    expnT_ = t[None, :] - t[:, None]
    expnT_[expnT_ < 0] = 1.0e6
    tpow_ = np.tile(t + 1.0, H)[None, :]
    HREP_ = np.zeros((H, D), dtype=np.float32)
    for h in range(H):
        HREP_[h, h * 128:(h + 1) * 128] = 1.0
    Esel_ = np.zeros((NE, RA), dtype=np.float32)
    for e in range(NE):
        Esel_[e, e * RANK:(e + 1) * RANK] = 1.0
    import ml_dtypes
    bf = lambda a: a.astype(ml_dtypes.bfloat16)
    return dict(
        identf=np.eye(128, dtype=np.float32),
        identb=bf(np.eye(128, dtype=np.float32)),
        onesrow_b=bf(np.ones((1, 128), np.float32)),
        onesrow_f=np.ones((1, 128), np.float32),
        ones128f=np.ones((128, 128), np.float32),
        onescol_f=np.ones((128, 1), np.float32),
        expnT=expnT_.astype(np.float32),
        tpow=tpow_.astype(np.float32),
        HREP=HREP_, Esel=bf(Esel_),
        cvals=np.tile(np.array([[1e-5, -0.5, 128.0, 0.5 / 64, 1.0 / 2048, 2.0, 0, 0]],
                                dtype=np.float32), (128, 1)),
    )


def make_inmaps(inputs):
    consts = _consts()
    x = np.ascontiguousarray(np.asarray(inputs["L0_attn_x"], dtype=np.float32))
    shards = np.split(x.reshape(16 * S, D), 8, axis=0)
    base = dict(
        wq=np.ascontiguousarray(inputs["L0_attn_wq"], dtype=np.float32),
        wk=np.ascontiguousarray(inputs["L0_attn_wk"], dtype=np.float32),
        wv=np.ascontiguousarray(inputs["L0_attn_wv"], dtype=np.float32),
        wg=np.ascontiguousarray(inputs["L0_attn_wg"], dtype=np.float32),
        wo=np.ascontiguousarray(inputs["L0_attn_wo"], dtype=np.float32),
        bq=np.ascontiguousarray(inputs["L0_attn_bq"], dtype=np.float32),
        bk=np.ascontiguousarray(inputs["L0_attn_bk"], dtype=np.float32),
        bv=np.ascontiguousarray(inputs["L0_attn_bv"], dtype=np.float32),
        bg=np.ascontiguousarray(inputs["L0_attn_bg"], dtype=np.float32),
        bo=np.ascontiguousarray(inputs["L0_attn_bo"], dtype=np.float32),
        gnw=np.ascontiguousarray(inputs["L0_attn_gn_w"], dtype=np.float32),
        gnb=np.ascontiguousarray(inputs["L0_attn_gn_b"], dtype=np.float32),
        gamma=np.ascontiguousarray(inputs["L0_attn_gamma"], dtype=np.float32),
        fAT=np.ascontiguousarray(
            np.asarray(inputs["L0_ffn_A"], dtype=np.float32)
            .reshape(NE, RANK, DT, 128).transpose(3, 2, 0, 1).reshape(128, DT * RA)),
        fBT=np.ascontiguousarray(
            np.asarray(inputs["L0_ffn_B"], dtype=np.float32)
            .reshape(4, 2, D, RANK).transpose(1, 3, 0, 2).reshape(128, 4 * D)),
        fWrT=np.ascontiguousarray(
            np.asarray(inputs["L0_ffn_Wr"], dtype=np.float32)
            .reshape(NE, DT, 128).transpose(2, 1, 0).reshape(128, DT * NE)),
        fbr=np.ascontiguousarray(np.asarray(inputs["L0_ffn_br"],
                                            dtype=np.float32).reshape(1, NE)),
        **consts,
    )
    return [dict(base, xs=np.ascontiguousarray(s)) for s in shards]


def kernel(**inputs):
    nc = build(stages=("P1", "P2", "P3", "P4", "P5"))
    in_maps = make_inmaps(inputs)
    res = run_bass_kernel_spmd(nc, in_maps, core_ids=list(range(8)))
    out = np.concatenate([r["y"] for r in res.results], axis=0)
    return out.reshape(16, S, D)


# version 28
# speedup vs baseline: 1.2185x; 1.0150x over previous
import contextlib
import numpy as np
import concourse.bass as bass
import concourse.mybir as mybir
from concourse.bass_utils import run_bass_kernel_spmd

F32 = mybir.dt.float32
BF16 = mybir.dt.bfloat16
ALU = mybir.AluOpType
AF = mybir.ActivationFunctionType
AX = mybir.AxisListType.X

D = 2048
DT = 16
NB = 2
S = 2048
NTOK = NB * S
NTT = NTOK // 128
H = 16
C = 128
NCH = S // C
RANK = 64
NE = 8
RA = NE * RANK
EPS = 1e-5
FSCALE = 0.5 / RANK

ENGINES = ("sync", "tensor", "vector", "scalar", "gpsimd")

STAGES = ("P1",)


class Sched:

    def __init__(self, nc):
        self.nc = nc
        self.steps = {e: [] for e in ENGINES}
        self.sems = {}
        self.cnt = {}
        self._ctx = contextlib.ExitStack()
        for s in ["T", "V", "Sc", "G"]:
            self._mksem(s)

    def _mksem(self, s):
        self.sems[s] = self._ctx.enter_context(self.nc.semaphore(s))
        self.cnt[s] = 0

    def at(self, name):
        if name not in self.cnt:
            self._mksem(name)
        return (name, self.cnt[name])

    def step(self, eng, fn, waits=(), inc=None, inc_n=1):
        w = tuple((s, v) for t in waits if t is not None for (s, v) in (t,) if v > 0)
        self.steps[eng].append((w, fn, inc, inc_n))
        if inc is not None:
            self.cnt[inc] += inc_n
            return (inc, self.cnt[inc])
        return None

    def dma(self, qeng, cls, out, in_, waits=(), **kw):
        if cls not in self.cnt:
            self._mksem(cls)
        def fn(e, out=out, in_=in_, kw=kw):
            return e.dma_start(out, in_, **kw)
        return self.step(qeng, fn, waits=waits, inc=cls, inc_n=16)

    def dmat(self, qeng, cls, out, in_, waits=()):
        if cls not in self.cnt:
            self._mksem(cls)
        def fn(e, out=out, in_=in_):
            return e.dma_start_transpose(out, in_)
        return self.step(qeng, fn, waits=waits, inc=cls, inc_n=16)

    def run(self, eng_name, eng):
        for waits, fn, inc, inc_n in self.steps[eng_name]:
            for s, v in waits:
                eng.wait_ge(self.sems[s], v)
            ins = fn(eng)
            if inc is not None:
                assert ins is not None, f"{eng_name} step with inc={inc} returned None"
                ins.then_inc(self.sems[inc], inc_n)

    def emit_and_close(self):
        with self.nc.Block() as block:
            for e in ENGINES:
                getattr(block, e)(lambda eng, e=e: self.run(e, eng))
        self._ctx.close()


def build(stages=STAGES, debug=()):
    nc = bass.Bass("TRN2", target_bir_lowering=False)

    def dram(name, shape, dtype, kind="Internal"):
        k = "ExternalOutput" if name in debug else kind
        return nc.dram_tensor(name, shape, dtype, kind=k).ap()

    EI = lambda n, s, d: nc.dram_tensor(n, s, d, kind="ExternalInput").ap()
    xs = EI("xs", [NTOK, D], F32)
    wq, wk, wv, wg, wo = (EI(n, [D, D], F32) for n in ["wq", "wk", "wv", "wg", "wo"])
    bq, bk, bv, bg, bo = (EI(n, [D], F32) for n in ["bq", "bk", "bv", "bg", "bo"])
    gnw, gnb = EI("gnw", [D], F32), EI("gnb", [D], F32)
    gamma = EI("gamma", [H], F32)
    fAT = EI("fAT", [128, DT * RA], F32)
    fBT = EI("fBT", [128, 4 * D], F32)
    fWrT = EI("fWrT", [128, DT * NE], F32)
    fbr = EI("fbr", [1, NE], F32)
    identf = EI("identf", [128, 128], F32)
    identb = EI("identb", [128, 128], BF16)
    onesrow_b = EI("onesrow_b", [1, 128], BF16)
    onesrow_f = EI("onesrow_f", [1, 128], F32)
    ones128f = EI("ones128f", [128, 128], F32)
    onescol_f = EI("onescol_f", [128, 1], F32)
    expnT = EI("expnT", [128, 128], F32)
    tpow = EI("tpow", [1, D], F32)
    HREP = EI("HREP", [H, D], F32)
    cvals = EI("cvals", [128, 8], F32)
    Esel = EI("Esel", [NE, RA], BF16)

    y = dram("y", [NTOK, D], F32, kind="ExternalOutput")
    wqB = dram("wqB", [D, D], BF16)
    wkB = dram("wkB", [D, D], BF16)
    wvB = dram("wvB", [D, D], BF16)
    wgB = dram("wgB", [D, D], BF16)
    woB = dram("woB", [D, D], BF16)
    x1t = dram("x1t", [NTOK, D], BF16)
    x1f = dram("x1f", [NTOK, D], F32)
    x2t = dram("x2t", [NTOK, D], BF16)
    x3f = dram("x3f", [NTOK, D], F32)
    x4t = dram("x4t", [NTOK, D], BF16)
    QT = dram("QT", [NB, NCH, DT, 128, C], BF16)
    KT = dram("KT", [NB, NCH, DT, 128, C], BF16)
    GT = dram("GT", [NB, NCH, DT, 128, C], BF16)
    ROT = dram("ROT", [NB, NCH, DT, 128, C], BF16)
    VT = dram("VT", [NB, NCH, DT, 128, C], BF16)

    sc = Sched(nc)
    A = nc.alloc_sbuf_tensor

    def _trmm(t, src_ap):
        last = None
        for h in range(H):
            last = t.matmul(ps_h[:, h * 128:(h + 1) * 128],
                            src_ap[:, h * 128:(h + 1) * 128], identb_sb[:],
                            start=(h % 4 == 0), stop=(h % 4 == 3))
        return last


    abuf = A("abuf", [128, DT * RA], BF16)
    bbuf = A("bbuf", [128, 4 * D], BF16)
    wrbuf = A("wrbuf", [128, DT * NE], BF16)
    dmask = A("dmask", [128, H * C], BF16)
    abc = A("abc", [128, D], BF16)
    bbcT = A("bbcT", [128, D], BF16)
    identb_sb = A("identb_sb", [128, 128], BF16)
    identf_sb = A("identf_sb", [128, 128], F32)
    onesrow_b_sb = A("onesrow_b_sb", [1, 128], BF16)
    onesrow_f_sb = A("onesrow_f_sb", [1, 128], F32)
    onescol_f_sb = A("onescol_f_sb", [128, 1], F32)
    esel_sb = A("esel_sb", [NE, RA], BF16)
    biases = A("biases", [128, 6 * DT], F32)
    biasrow = A("biasrow", [1, D], BF16)
    brrow = A("brrow", [1, NE], BF16)
    gamrow = A("gamrow", [1, H], F32)
    lngrow = A("lngrow", [1, H], F32)
    lngcol16 = A("lngcol16", [H, 1], F32)
    lncol = A("lncol", [128, H], F32)
    gccol = A("gccol", [128, H], F32)
    gcbc = A("gcbc", [128, D], F32)
    stats = A("stats", [128, 256], F32)
    cvals_sb = A("cvals_sb", [128, 8], F32)
    CVALS = [EPS, -0.5, float(C), FSCALE, 1.0 / D, 2.0]
    for ci, cv in enumerate(CVALS):
        nc.const_aps.register(F32, cv, cvals_sb[:, ci:ci + 1])

    GL = 1024
    GLH = 512
    NG = NTOK // GL
    xTg = A("xTg", [128, DT * GL], BF16)
    dtmp = A("dtmp", [128, 128], F32)
    expnT_sb = A("expnT_sb", [128, 128], F32)
    ones128f_sb = A("ones128f_sb", [128, 128], F32)
    WSTG = [A(f"WSTG{i}", [128, 2 * D], BF16) for i in range(2)]
    wtile = [A(f"wtile{i}", [128, DT * 128], BF16) for i in range(2)]
    xtout2 = wtile
    qkstage = [A(f"qkst{i}", [128, 1024], BF16) for i in range(2)]
    wtile3 = [wtile[0][:], wtile[1][:]]
    xbuf = [A(f"xbuf{i}", [128, D], F32) for i in range(2)]
    sjunk = A("sjunk", [128, D], BF16)
    xtout = [A(f"xtout{i}", [128, D], BF16) for i in range(2)]
    hbuf = [A(f"hbuf{i}", [128, 4 * 512], BF16) for i in range(2)]
    mexp = [A(f"mexp{i}", [128, 4 * 512], BF16) for i in range(2)]
    mwT = [A(f"mwT{i}", [NE, 512], BF16) for i in range(2)]
    mwtile = [A(f"mwtile{i}", [128, NE], F32) for i in range(4)]
    etmp = [A(f"etmp{i}", [128, NE], F32) for i in range(4)]
    mtmp = [A(f"mtmp{i}", [128, NE], F32) for i in range(4)]
    xres = [A(f"xres{i}", [128, D], F32) for i in range(2)]
    oub = [A(f"oub{i}", [128, D], F32) for i in range(2)]
    hrep_sb = oub[0][0:16, 0:D]
    lngexp = oub[1][0:1, 0:D]
    r1row = xres[0][0:1, 0:D]
    tpow_sb = xres[1][0:1, 0:D]

    P = nc.alloc_psum_tensor
    ps_h = P("ps_h", [128, 4 * 512], F32)
    ps_r45 = P("ps_r45", [128, 1024], F32)
    ps_sel = P("ps_sel", [128, 1024], F32)

    for dst, src in [(identb_sb[:], identb), (identf_sb[:], identf),
                     (onesrow_b_sb[:], onesrow_b), (onesrow_f_sb[:], onesrow_f),
                     (onescol_f_sb[:], onescol_f), (ones128f_sb[:], ones128f),
                     (expnT_sb[:], expnT), (esel_sb[:], Esel),
                     (gamrow[:], gamma.rearrange("(o h) -> o h", o=1)),
                     (cvals_sb[:], cvals),
                     (hrep_sb, HREP), (tpow_sb, tpow)]:
        sc.dma("sync", "c0", dst, src)
    for i, b in enumerate([bq, bk, bg, gnw, gnb, bv]):
        sc.dma("sync", "c0", biases[:, i * DT:(i + 1) * DT],
               b.rearrange("(t p) -> p t", p=128), allow_slow_non_contiguous=True)
    sc.dma("sync", "c0", lngcol16[:], gamma.rearrange("(h o) -> h o", o=1),
           allow_slow_non_contiguous=True)
    cl = sc.at("c0")

    borow_tok = sc.dma("gpsimd", "bo0", biasrow[:], bo.rearrange("(o d) -> o d", o=1))
    brrow_tok = sc.dma("gpsimd", "br0", brrow[:], fbr)
    abT = sc.dma("gpsimd", "prep", abuf[:], fAT)
    bbT = sc.dma("gpsimd", "prep", bbuf[:], fBT, waits=[abT])
    prep_tok = sc.dma("gpsimd", "prep", wrbuf[:], fWrT, waits=[bbT])

    r2f = lambda ap, p=128: ap.rearrange("(a p) d -> p a d", p=p)
    stg3 = [WSTG[i].rearrange("p (a d) -> p a d", d=D) for i in range(2)]
    prev_st = [None, None]
    wprep = {}
    for w, wB, nm in [(wq, wqB, "q"), (wk, wkB, "k"), (wv, wvB, "v"),
                      (wg, wgB, "g"), (wo, woB, "o")]:
        wr3 = r2f(w)
        wB3 = r2f(wB)
        for qtr in range(8):
            pp = qtr % 2
            ld = sc.dma("gpsimd", f"wc{pp}", stg3[pp][:, 0:2, :],
                        wr3[:, qtr * 2:(qtr + 1) * 2, :],
                        waits=[prev_st[pp]] if prev_st[pp] else [])
            prev_st[pp] = sc.dma("gpsimd", f"wc{pp}", wB3[:, qtr * 2:(qtr + 1) * 2, :],
                                 stg3[pp][:, 0:2, :], waits=[ld])
        wprep[nm] = (prev_st[0], prev_st[1])

    def emit_prep_tables():
        s1 = sc.step("scalar", lambda s: s.activation(lngrow[:], gamrow[:], AF.Ln),
                     waits=[cl], inc="Sc")
        s2 = sc.step("scalar", lambda s: s.activation(lngcol16[:], lngcol16[:], AF.Ln),
                     waits=[cl], inc="Sc")
        m1 = sc.step("tensor", lambda t: t.matmul(ps_h[:, 0:H], onesrow_f_sb[:],
                                                  lngrow[:], start=True, stop=True),
                     waits=[s1], inc="T")
        v1 = sc.step("vector", lambda v: v.tensor_copy(lncol[:], ps_h[:, 0:H]),
                     waits=[m1], inc="V")
        prev_ev = None
        for j in range(4):
            def fmm(t, j=j):
                return t.matmul(ps_r45[0:1, 0:512], lngcol16[:],
                                hrep_sb[:, j * 512:(j + 1) * 512], start=True, stop=True)
            mmj = sc.step("tensor", fmm, waits=[s2, cl] + ([prev_ev] if prev_ev else []),
                          inc="T")
            def fev2(v, j=j):
                return v.tensor_copy(lngexp[:, j * 512:(j + 1) * 512], ps_r45[0:1, 0:512])
            prev_ev = sc.step("vector", fev2, waits=[mmj], inc="V")
        lgdone = prev_ev
        vr1 = sc.step("vector", lambda v: v.tensor_tensor(r1row, lngexp, tpow_sb,
                                                          ALU.mult),
                      waits=[lgdone], inc="V")
        sr1 = sc.step("scalar", lambda s: s.activation(r1row, r1row, AF.Exp),
                      waits=[vr1], inc="Sc")
        sgc = sc.step("scalar", lambda s: s.activation(gccol[:], lncol[:], AF.Exp,
                                                       scale=float(C)),
                      waits=[v1], inc="Sc")
        abc_mm_last = None
        for dst, row, dep in [(abc, r1row, sr1)]:
            for j in range(4):
                def fmm(t, row=row, j=j):
                    return t.matmul(ps_r45[:, 512:1024], onesrow_f_sb[:],
                                    row[:, j * 512:(j + 1) * 512], start=True, stop=True)
                abc_mm_last = sc.step("tensor", fmm, waits=[dep, sc.at("V")], inc="T")
                def fev3(v, dst=dst, j=j):
                    return v.tensor_copy(dst[:, j * 512:(j + 1) * 512], ps_r45[:, 512:1024])
                sc.step("vector", fev3, waits=[abc_mm_last], inc="V")
        for h in range(H):
            sc.step("vector", lambda v, h=h: v.tensor_scalar(
                gcbc[:, h * 128:(h + 1) * 128], ones128f_sb[:], gccol[:, h:h + 1],
                None, ALU.mult), waits=[sgc], inc="V")
        abc_done = sc.at("V")
        prev_bb = None
        for h in range(H):
            vrcp = sc.step("vector", lambda v, h=h: v.reciprocal(
                dtmp[:], abc[:, h * 128:(h + 1) * 128]),
                waits=[abc_done] if h == 0 else [prev_bb], inc="V")
            prev_bb = sc.step("vector", lambda v, h=h: v.tensor_scalar(
                bbcT[:, h * 128:(h + 1) * 128], dtmp[:],
                gccol[:, h:h + 1], None, ALU.mult), waits=[vrcp, sgc], inc="V")
        for h in range(H):
            va = sc.step("vector", lambda v, h=h: v.tensor_scalar(
                dtmp[:], expnT_sb[:], lncol[:, h:h + 1], None, ALU.mult),
                waits=[v1, cl, sc.at("Sc"), prev_bb], inc="V")
            sc.step("scalar", lambda s, h=h: s.activation(
                dmask[:, h * 128:(h + 1) * 128], dtmp[:], AF.Exp), waits=[va], inc="Sc")
        return [abc_mm_last, sr1]

    def emit_ln_pass(src_ap, dst_ap, scol0, first_waits=(), tile_waits=None):
        stores = []
        applies = []

        def emit_tile(i):
            bi = i % 2
            w = list(first_waits) if i == 0 else []
            if i >= 2:
                w.append(applies[i - 2])
            if tile_waits is not None:
                w.append(tile_waits[i])
            ld = sc.dma("sync", f"lx{bi}", xbuf[bi][:],
                        src_ap[i * 128:(i + 1) * 128, :], waits=w)
            c0 = scol0 + (i % 4) * 8
            sq = sc.step("scalar", lambda s, bi=bi, c0=c0: s.activation(
                sjunk[:], xbuf[bi][:], AF.Square, accum_out=stats[:, c0 + 1:c0 + 2]),
                waits=[ld] + ([applies[i - 4]] if i >= 4 else []), inc="Sc")
            rv = sc.step("vector", lambda v, bi=bi, c0=c0: v.reduce_sum(
                stats[:, c0:c0 + 1], xbuf[bi][:], axis=AX),
                waits=[ld] + ([applies[i - 4]] if i >= 4 else []), inc="V")
            vmu = sc.step("vector", lambda v, c0=c0: v.tensor_scalar(
                stats[:, c0 + 2:c0 + 3], stats[:, c0:c0 + 1], 1.0 / D, None, ALU.mult),
                waits=[rv], inc="V")
            vm2_ = sc.step("vector", lambda v, c0=c0: v.tensor_tensor(
                stats[:, c0 + 3:c0 + 4], stats[:, c0 + 2:c0 + 3],
                stats[:, c0 + 2:c0 + 3], ALU.mult), waits=[vmu], inc="V")
            vva = sc.step("vector", lambda v, c0=c0: v.scalar_tensor_tensor(
                stats[:, c0 + 4:c0 + 5], stats[:, c0 + 1:c0 + 2], 1.0 / D,
                stats[:, c0 + 3:c0 + 4], ALU.mult, ALU.subtract),
                waits=[vm2_, sq], inc="V")
            sl = sc.step("scalar", lambda s, c0=c0: s.activation(
                stats[:, c0 + 5:c0 + 6], stats[:, c0 + 4:c0 + 5], AF.Ln, bias=EPS),
                waits=[vva], inc="Sc")
            se = sc.step("scalar", lambda s, c0=c0: s.activation(
                stats[:, c0 + 6:c0 + 7], stats[:, c0 + 5:c0 + 6], AF.Exp, scale=-0.5),
                waits=[sl], inc="Sc")
            ob = i % 2
            ap_w = [se]
            if i >= 2:
                ap_w.append(stores[i - 2])
            vap = sc.step("vector", lambda v, bi=bi, ob=ob, c0=c0: v.tensor_scalar(
                xtout[ob][:], xbuf[bi][:], stats[:, c0 + 2:c0 + 3],
                stats[:, c0 + 6:c0 + 7], ALU.subtract, ALU.mult),
                waits=ap_w, inc="V")
            applies.append(vap)
            stores.append(sc.dma("scalar", f"sx{ob}", dst_ap[i * 128:(i + 1) * 128, :],
                                 xtout[ob][:], waits=[vap]))
        return emit_tile, stores

    def emit_ffn_pass(xt_ap, res_ap, out_ap, xt_stores, res_tile_waits=None,
                      ln_dst=None, scol0=64, entry_waits=()):
        out_stores = [None] * NTT
        ln_stores = [None] * NTT if ln_dst is not None else None
        ln_applies = [None] * NTT
        qsc0 = [sc.at("sx0"), sc.at("sx1")]
        hev_last = sc.at("V")
        sel_last_by_chunk = []
        r45_free = [sc.at("V"), sc.at("V")]
        sel_free = [sc.at("V"), sc.at("V")]
        mw_ev_prev = [sc.at("V")] * 4
        me_mm_by_cb = [None, None]
        res_applies = []
        hev_by_chunk = []

        def emit_group(tg):
            nonlocal hev_last, r45_free, sel_free
            tla = tlb = None
            for kt in range(DT):
                w = list(entry_waits)
                if xt_stores is not None:
                    w.append(xt_stores[tg * 8 + 7])
                    w.append(xt_stores[tg * 8 + 6])
                if tg > 0:
                    w.append(sc.at("T"))
                q, clsn = ("sync", "tl") if kt < 8 else ("sync", "tlb")
                t_ = sc.dmat(q, clsn, xTg[:, kt * GL:(kt + 1) * GL],
                             xt_ap[tg * GL:(tg + 1) * GL, kt * 128:(kt + 1) * 128],
                             waits=w)
                if kt < 8:
                    tla = t_
                else:
                    tlb = t_
            for ch in range(GL // 512):
                gch = tg * (GL // 512) + ch
                cb = gch % 2
                t0 = tg * GL + ch * 512
                w = [tla, tlb, r45_free[0]] + list(entry_waits)
                def frt(t, ch=ch):
                    last = None
                    for tt in range(4):
                        L = ps_r45[:, tt * 128:tt * 128 + NE]
                        t.matmul(L, onesrow_b_sb[:], brrow[:],
                                 start=(tt == 0), stop=False)
                        for kt in range(DT):
                            last = t.matmul(
                                L, xTg[:, kt * GL + ch * 512 + tt * 128:
                                       kt * GL + ch * 512 + (tt + 1) * 128],
                                wrbuf[:, kt * NE:(kt + 1) * NE],
                                start=False, stop=(tt == 3 and kt == DT - 1))
                    return last
                rmm = sc.step("tensor", frt, waits=w, inc="T")
                w = [tla, tlb, hev_last] + list(entry_waits)
                def fh(t, ch=ch):
                    last = None
                    for mt in range(4):
                        for kt in range(DT):
                            last = t.matmul(
                                ps_h[:, mt * 512:(mt + 1) * 512],
                                abuf[:, kt * RA + mt * 128:kt * RA + (mt + 1) * 128],
                                xTg[:, kt * GL + ch * 512:kt * GL + (ch + 1) * 512],
                                start=(kt == 0), stop=(kt == DT - 1))
                    return last
                hmm = sc.step("tensor", fh, waits=w, inc="T")
                vmw_l, vmk_l, sex_l = [], [], []
                for tt in range(4):
                    L = ps_r45[:, tt * 128:tt * 128 + NE]
                    c0 = 96 + ((gch % 2) * 4 + tt) * 4
                    vmx = sc.step("vector", lambda v, L=L, c0=c0: v.reduce_max(
                        stats[:, c0:c0 + 1], L, axis=AX, negate=True),
                        waits=[rmm], inc="V")
                    sex = sc.step("scalar", lambda s, L=L, tt=tt, c0=c0: s.activation(
                        etmp[tt][:], L, AF.Exp, bias=stats[:, c0:c0 + 1]),
                        waits=[vmx], inc="Sc")
                    vsm = sc.step("vector", lambda v, tt=tt, c0=c0: v.reduce_sum(
                        stats[:, c0 + 1:c0 + 2], etmp[tt][:], axis=AX),
                        waits=[sex], inc="V")
                    vrc = sc.step("vector", lambda v, c0=c0: v.reciprocal(
                        stats[:, c0 + 2:c0 + 3], stats[:, c0 + 1:c0 + 2]),
                        waits=[vsm], inc="V")
                    vsc = sc.step("vector", lambda v, c0=c0: v.tensor_scalar(
                        stats[:, c0 + 3:c0 + 4], stats[:, c0 + 2:c0 + 3],
                        FSCALE, None, ALU.mult), waits=[vrc], inc="V")
                    vmk = sc.step("vector", lambda v, L=L, tt=tt, c0=c0: v.tensor_scalar(
                        mtmp[tt][:], L, stats[:, c0:c0 + 1], 0.0, ALU.add,
                        ALU.is_equal), waits=[vsc], inc="V")
                    vmw = sc.step("vector", lambda v, tt=tt, c0=c0: v.tensor_scalar(
                        mwtile[tt][:], mtmp[tt][:], stats[:, c0 + 3:c0 + 4],
                        None, ALU.mult), waits=[vmk, mw_ev_prev[tt]], inc="V")
                    vmw_l.append(vmw)
                    vmk_l.append(vmk)
                    sex_l.append(sex)
                w2 = [vmw_l[3], sel_free[0]]
                def ftmw(t):
                    last = None
                    for tt in range(4):
                        last = t.matmul(ps_sel[0:8, tt * 128:(tt + 1) * 128],
                                        mwtile[tt][:], identf_sb[:],
                                        start=(tt == 0), stop=(tt == 3))
                    return last
                tmwg = sc.step("tensor", ftmw, waits=w2, inc="T")
                for tt in range(4):
                    mw_ev_prev[tt] = tmwg
                evw = [tmwg]
                if me_mm_by_cb[cb] is not None:
                    evw.append(me_mm_by_cb[cb])
                ev = sc.step("scalar", lambda s, cb=cb: s.copy(
                    mwT[cb][:], ps_sel[0:8, 0:512]), waits=evw, inc="Sc")
                me_evs = []
                me_banks = [ps_r45[:, 512:1024], ps_r45[:, 0:512]]
                for rp in range(4):
                    bk_ = rp % 2
                    w = [ev]
                    if bk_ == 1:
                        w += [vmk_l[3], sex_l[3]]
                    else:
                        w.append(r45_free[1])
                    if rp >= 2:
                        w.append(me_evs[rp - 2])
                    def fme(t, rp=rp, bk_=bk_, cb=cb):
                        return t.matmul(me_banks[bk_],
                                        esel_sb[:, rp * 128:(rp + 1) * 128],
                                        mwT[cb][:], start=True, stop=True)
                    mm = sc.step("tensor", fme, waits=w, inc="T")
                    if rp == 3:
                        me_mm_by_cb[cb] = mm
                    me_evs.append(sc.step("scalar", lambda s, rp=rp, bk_=bk_, cb=cb:
                                          s.copy(
                                              mexp[cb][:, rp * 512:(rp + 1) * 512],
                                              me_banks[bk_]),
                                          waits=[mm], inc="Sc"))
                r45_free = [me_evs[3], me_evs[2]]
                hev = None
                for mt in range(4):
                    w = [hmm, me_evs[mt]]
                    if len(sel_last_by_chunk) >= 2:
                        w.append(sel_last_by_chunk[-2])
                    hev = sc.step("vector", lambda v, mt=mt, cb=cb: v.tensor_tensor(
                        hbuf[cb][:, mt * 512:(mt + 1) * 512],
                        ps_h[:, mt * 512:(mt + 1) * 512],
                        mexp[cb][:, mt * 512:(mt + 1) * 512], ALU.mult),
                        waits=w, inc="V")
                hev_last = hev
                hev_by_chunk.append(hev)
                for tt in range(4):
                    i = (t0 // 128) + tt
                    rb = i % 2
                    w = []
                    if len(res_applies) >= 2:
                        w.append(res_applies[-2])
                    else:
                        w.extend(entry_waits)
                    if res_tile_waits is not None:
                        w.append(res_tile_waits[i])
                    rld = sc.dma("sync", f"rx{rb}", xres[rb][:],
                                 res_ap[i * 128:(i + 1) * 128, :], waits=w)
                    ob = i % 2
                    qevs = []
                    for oq in range(4):
                        bk_ = oq % 2
                        w = [hev, sel_free[bk_]]
                        if oq >= 2:
                            w.append(qevs[oq - 2])
                        def fsel(t, tt=tt, oq=oq, bk_=bk_, cb=cb):
                            last = None
                            for rp in range(4):
                                last = t.matmul(
                                    ps_sel[:, bk_ * 512:(bk_ + 1) * 512],
                                    hbuf[cb][:, rp * 512 + tt * 128:
                                              rp * 512 + (tt + 1) * 128],
                                    bbuf[:, rp * D + oq * 512:rp * D + (oq + 1) * 512],
                                    start=(rp == 0), stop=(rp == 3))
                            return last
                        smm = sc.step("tensor", fsel, waits=w, inc="T")
                        wq_ = [smm, rld]
                        if i >= 2 and out_stores[i - 2]:
                            wq_.append(out_stores[i - 2])
                        elif i < 2:
                            wq_.extend(entry_waits)
                        if ln_dst is not None and i >= 2 and ln_applies[i - 2]:
                            wq_.append(ln_applies[i - 2])
                        qev = sc.step("vector", lambda v, oq=oq, bk_=bk_, rb=rb, ob=ob:
                                      v.tensor_tensor(
                                          oub[ob][:, oq * 512:(oq + 1) * 512],
                                          ps_sel[:, bk_ * 512:(bk_ + 1) * 512],
                                          xres[rb][:, oq * 512:(oq + 1) * 512],
                                          ALU.add),
                                      waits=wq_, inc="V")
                        qevs.append(qev)
                        sel_free[bk_] = qev
                    sel_last_by_chunk.append(smm) if tt == 3 else None
                    res_applies.append(qevs[-1])
                    ost = sc.dma("scalar", f"so{ob}", out_ap[i * 128:(i + 1) * 128, :],
                                 oub[ob][:], waits=[qevs[-1]])
                    out_stores[i] = ost
                    if ln_dst is not None:
                        c0 = scol0 + (i % 4) * 8
                        sq = sc.step("scalar", lambda s, ob=ob, c0=c0: s.activation(
                            sjunk[:], oub[ob][:], AF.Square,
                            accum_out=stats[:, c0 + 1:c0 + 2]),
                            waits=[qevs[-1]], inc="Sc")
                        rv = sc.step("vector", lambda v, ob=ob, c0=c0: v.reduce_sum(
                            stats[:, c0:c0 + 1], oub[ob][:], axis=AX),
                            waits=[qevs[-1]], inc="V")
                        vmu = sc.step("vector", lambda v, c0=c0: v.tensor_scalar(
                            stats[:, c0 + 2:c0 + 3], stats[:, c0:c0 + 1], 1.0 / D,
                            None, ALU.mult), waits=[rv], inc="V")
                        vm2_ = sc.step("vector", lambda v, c0=c0: v.tensor_tensor(
                            stats[:, c0 + 3:c0 + 4], stats[:, c0 + 2:c0 + 3],
                            stats[:, c0 + 2:c0 + 3], ALU.mult), waits=[vmu], inc="V")
                        vva = sc.step("vector", lambda v, c0=c0: v.scalar_tensor_tensor(
                            stats[:, c0 + 4:c0 + 5], stats[:, c0 + 1:c0 + 2], 1.0 / D,
                            stats[:, c0 + 3:c0 + 4], ALU.mult, ALU.subtract),
                            waits=[vm2_, sq], inc="V")
                        sl = sc.step("scalar", lambda s, c0=c0: s.activation(
                            stats[:, c0 + 5:c0 + 6], stats[:, c0 + 4:c0 + 5],
                            AF.Ln, bias=EPS), waits=[vva], inc="Sc")
                        se = sc.step("scalar", lambda s, c0=c0: s.activation(
                            stats[:, c0 + 6:c0 + 7], stats[:, c0 + 5:c0 + 6],
                            AF.Exp, scale=-0.5), waits=[sl], inc="Sc")
                        lw = [se]
                        if i >= 2 and ln_stores[i - 2]:
                            lw.append(ln_stores[i - 2])
                        else:
                            lw.extend(qsc0)
                        vap = sc.step("vector", lambda v, ob=ob, c0=c0: v.tensor_scalar(
                            xtout2[ob][:], oub[ob][:], stats[:, c0 + 2:c0 + 3],
                            stats[:, c0 + 6:c0 + 7], ALU.subtract, ALU.mult),
                            waits=lw, inc="V")
                        ln_applies[i] = vap
                        ln_stores[i] = sc.dma("scalar", f"sx2{ob}",
                                              ln_dst[i * 128:(i + 1) * 128, :],
                                              xtout2[ob][:], waits=[vap])
        return emit_group, out_stores, ln_stores

    xtg_free = emit_prep_tables()
    ln_tile, p1a_stores = emit_ln_pass(xs, x1t, 0, first_waits=[cl])
    setup_done = [sc.at("V"), sc.at("Sc"), sc.at("T"), sc.at("bo0"),
                  sc.at("br0"),
                  sc.at("c0"), sc.at("prep")]
    ffn_entry = [brrow_tok, prep_tok] + xtg_free
    for i in range(8):
        ln_tile(i)
    ffn_g1, x1_stores, x2t_stores = emit_ffn_pass(x1t, xs, x1f, p1a_stores,
                                                  ln_dst=x2t, scol0=64,
                                                  entry_waits=ffn_entry)
    for g in range(NG):
        ffn_g1(g)
        for i in range(8 * (g + 1), min(8 * (g + 2), NTT)):
            ln_tile(i)

    proj_final = {}
    ro_store = {}
    if "P2" in stages:
        psh2 = [ps_r45[:], ps_sel[:]]
        w3 = {"q": wqB, "k": wkB, "v": wvB, "g": wgB}
        dst = {"q": QT, "k": KT, "v": VT, "g": GT}
        bcol = {"q": 0, "k": 1, "v": 5, "g": 2}
        p2_V0 = sc.at("V")
        prev_mm = {0: None, 1: None}
        prev_store = {0: None, 1: None}
        prev_ev = [p2_V0, p2_V0]
        cur_tl = [None]

        def emit_p2_tloads(tg):
            guard = [t for t in (prev_mm[0], prev_mm[1]) if t] or [sc.at("T")]
            tl2 = [None, None]
            for kt in range(DT):
                w = [x2t_stores[tg * 8 + 7], x2t_stores[tg * 8 + 6]] + guard
                q, clsn, slot = ("sync", "tl", 0) if kt < 8 else ("sync", "tlb", 1)
                tl2[slot] = sc.dmat(q, clsn, xTg[:, kt * GL:(kt + 1) * GL],
                                    x2t[tg * GL:(tg + 1) * GL,
                                        kt * 128:(kt + 1) * 128],
                                    waits=w)
            cur_tl[0] = tl2

        def emit_p2_m(nm, tg, m):
            tl2 = cur_tl[0]
            b0 = tg // 2
            ch0 = (tg % 2) * 8
            wB3 = w3[nm].rearrange("(a p) d -> p a d", p=128)
            gi = m % 2
            wwt = list(wprep[nm])
            if prev_mm[gi] is not None:
                wwt.append(prev_mm[gi])
            else:
                wwt += [x2t_stores[NTT - 1], x2t_stores[NTT - 2]]
            wld = sc.dma("sync", f"wt{gi}",
                         wtile[gi].rearrange("p (a d) -> p a d", d=128),
                         wB3[:, :, m * 128:(m + 1) * 128], waits=wwt)
            hb = m % 2
            def fp2(t, gi=gi, hb=hb):
                last = None
                for kt in range(DT):
                    for n2 in range(2):
                        last = t.matmul(
                            psh2[hb][:, n2 * 512:(n2 + 1) * 512],
                            wtile[gi][:, kt * 128:(kt + 1) * 128],
                            xTg[:, kt * GL + n2 * 512:kt * GL + (n2 + 1) * 512],
                            start=(kt == 0), stop=(kt == DT - 1))
                return last
            mm = sc.step("tensor", fp2, waits=tl2 + [wld, prev_ev[hb]], inc="T")
            prev_mm[gi] = mm
            sp = m % 2
            wev = [mm]
            if prev_store[sp] is not None:
                wev.append(prev_store[sp])
            if nm == "g":
                ev = sc.step("scalar", lambda s, m=m, hb=hb, sp=sp: s.activation(
                    qkstage[sp][:], psh2[hb][:], AF.Silu,
                    bias=biases[:, bcol["g"] * DT + m:bcol["g"] * DT + m + 1]),
                    waits=wev, inc="Sc")
            else:
                ev = sc.step("vector", lambda s, m=m, hb=hb, sp=sp, nm=nm:
                             s.tensor_scalar(
                                 qkstage[sp][:], psh2[hb][:],
                                 biases[:, bcol[nm] * DT + m:bcol[nm] * DT + m + 1],
                                 None, ALU.add), waits=wev, inc="V")
            prev_ev[hb] = ev
            st = sc.dma("scalar", f"sqk{sp}",
                        dst[nm][b0, ch0:ch0 + 8, m].rearrange("c p t -> p c t"),
                        qkstage[sp].rearrange("p (c t) -> p c t", t=C),
                        waits=[ev])
            prev_store[sp] = st
            if m == DT - 1:
                proj_final[(nm, tg)] = (prev_store[0], prev_store[1])

    if "P3" in stages:
        xbufB = [xb.bitcast(BF16) for xb in xbuf]
        xresB = [xr.bitcast(BF16) for xr in xres]
        qch = [xbufB[0][:, 0:D], xbufB[0][:, D:2 * D]]
        kch = [xbufB[1][:, 0:D], xbufB[1][:, D:2 * D]]
        vtch = [xresB[0][:, 0:D], xresB[0][:, D:2 * D]]
        vtok = [xresB[1][:, 0:D], xresB[1][:, D:2 * D]]
        khatT = [hbuf[0][:], hbuf[1][:]]
        khtok = [mexp[0][:], mexp[1][:]]
        smask = [WSTG[0][:, 0:D], WSTG[0][:, D:2 * D]]
        retsb = [WSTG[1][:, 0:D], WSTG[1][:, D:2 * D]]
        qhat = qch
        st_f = [oub[0], oub[1]]
        st_b = [xtout[0], xtout[1]]
        statacc = stats[:, 128:192]

        p3v0 = sc.at("V")
        p3s0 = sc.at("Sc")
        mz = sc.step("vector", lambda v: v.memset(statacc, 0.0), waits=[p3v0],
                     inc="V")
        stz = [sc.step("vector", lambda v, b=b: v.memset(st_f[b][:], 0.0),
                       waits=[p3v0, sc.at("so0"), sc.at("so1")], inc="V")
               for b in range(NB)]
        t3 = dict(ld_gate=[(), ()], sppr=[None, None],
                  psh_free=[sc.at("Sc"), sc.at("V"), p3v0],
                  prev_cast=[None, None], prev_sro=[None, None],
                  prev_stats=None, prev_gsm=None, prev_wst=[(), ()])

        def p3_unit_gen(c, b):
            it = c * 2 + b
            cb = it % 2
            pf = lambda nm_: list(proj_final.get((nm_, b * 2 + c // 8), ()))
            gate_ = list(t3["ld_gate"][cb])
            r3 = lambda ap: ap.rearrange("dt p t -> p dt t")
            s3 = lambda ap: ap.rearrange("p (dt t) -> p dt t", t=C)
            lq = sc.dma("gpsimd", f"q3{cb}", s3(qch[cb]), r3(QT[b, c]),
                        waits=pf("q") + gate_)
            lk = sc.dma("gpsimd", f"k3{cb}", s3(kch[cb]), r3(KT[b, c]),
                        waits=pf("k") + gate_)
            lv = sc.dma("gpsimd", f"v3{cb}", s3(vtch[cb]), r3(VT[b, c]),
                        waits=pf("v") + gate_)
            wg_ = [t3["sppr"][cb]] if t3["sppr"][cb] else []
            if it < 2:
                wg_ = wg_ + [setup_done[0]]
            def fs2(t, cb=cb):
                last = None
                for h in range(H):
                    last = t.matmul(ps_h[:, h * 128:(h + 1) * 128],
                                    kch[cb][:, h * 128:(h + 1) * 128],
                                    qch[cb][:, h * 128:(h + 1) * 128],
                                    start=(h % 4 == 0), stop=(h % 4 == 3))
                return last
            s2mm = sc.step("tensor", fs2, waits=[lq, lk] + t3["psh_free"], inc="T")
            gqh = sc.step("gpsimd", lambda g, cb=cb: g.tensor_tensor(
                qhat[cb], qch[cb], abc[:], ALU.mult), waits=[s2mm] + wg_, inc="G")
            gkh = sc.step("gpsimd", lambda g, cb=cb: g.tensor_tensor(
                khatT[cb], kch[cb], bbcT[:], ALU.mult), waits=[s2mm, lk] + wg_,
                inc="G")
            yield
            wsc = [s2mm]
            if t3["prev_gsm"] is not None:
                wsc.append(t3["prev_gsm"])
            scp = sc.step("scalar", lambda s: s.copy(sjunk[:], ps_h[:]),
                          waits=wsc, inc="Sc")
            wsm = [scp] + ([t3["sppr"][cb]] if t3["sppr"][cb] else [])
            if it < 2:
                wsm += [wprep["o"][0], wprep["o"][1]]
            gsm = sc.step("gpsimd", lambda g, cb=cb: g.tensor_tensor(
                smask[cb], sjunk[:], dmask[:], ALU.mult), waits=wsm, inc="G")
            t3["prev_gsm"] = gsm
            trk = sc.step("tensor", lambda t, cb=cb: _trmm(t, khatT[cb]),
                          waits=[gkh, scp], inc="T")
            ekt = sc.step("vector", lambda v, cb=cb: v.tensor_copy(
                khtok[cb], ps_h[:]),
                waits=[trk] + ([t3["sppr"][cb]] if t3["sppr"][cb] else []), inc="V")
            yield
            trv = sc.step("tensor", lambda t, cb=cb: _trmm(t, vtch[cb]),
                          waits=[lv, ekt], inc="T")
            evt = sc.step("scalar", lambda s, cb=cb: s.copy(vtok[cb], ps_h[:]),
                          waits=[trv] + ([t3["sppr"][cb]] if t3["sppr"][cb] else []),
                          inc="Sc")
            yield
            def fret(t, cb=cb, b=b, c=c):
                last = None
                for h in range(H):
                    last = t.matmul(ps_h[:, h * 128:(h + 1) * 128],
                                    vtok[cb][:, h * 128:(h + 1) * 128],
                                    smask[cb][:, h * 128:(h + 1) * 128],
                                    start=(h % 4 == 0),
                                    stop=(h % 4 == 3 and c == 0))
                    if c > 0:
                        last = t.matmul(ps_h[:, h * 128:(h + 1) * 128],
                                        st_b[b][:, h * 128:(h + 1) * 128],
                                        qhat[cb][:, h * 128:(h + 1) * 128],
                                        start=False, stop=(h % 4 == 3))
                return last
            wret = [evt, gsm, gqh]
            if c > 0:
                wret.append(t3["prev_cast"][b])
            rmm3 = sc.step("tensor", fret, waits=wret, inc="T")
            wer = [rmm3]
            if it < 2:
                wer += [wprep["o"][0], wprep["o"][1]]
            if t3["prev_sro"][cb] is not None:
                wer.append(t3["prev_sro"][cb])
            er1 = sc.step("scalar", lambda s, cb=cb: s.copy(
                retsb[cb][:, 0:1024], ps_h[:, 0:1024]), waits=wer, inc="Sc")
            er2 = sc.step("scalar", lambda s, cb=cb: s.copy(
                retsb[cb][:, 1024:2048], ps_h[:, 1024:2048]), waits=[rmm3],
                inc="Sc")
            sro = sc.dma("gpsimd", f"sro{cb}",
                         ROT[b, c].rearrange("dt p t -> p dt t"),
                         retsb[cb].rearrange("p (dt t) -> p dt t", t=C),
                         waits=[er2, er1] +
                         ([t3["prev_sro"][cb]] if t3["prev_sro"][cb] else []))
            t3["prev_sro"][cb] = sro
            pstat = t3["prev_stats"]
            vst1 = sc.step("vector", lambda v, cb=cb: v.reduce_sum(
                stats[:, 224:240], retsb[cb].rearrange("p (s t) -> p s t", s=16),
                axis=AX), waits=[er1, er2] + ([pstat] if pstat else []), inc="V")
            vst2 = sc.step("vector", lambda v, b=b: v.tensor_tensor(
                statacc[:, b * 16:(b + 1) * 16], statacc[:, b * 16:(b + 1) * 16],
                stats[:, 224:240], ALU.add), waits=[vst1, mz], inc="V")
            def fsq(s, cb=cb):
                last = None
                for h in range(H):
                    last = s.activation(dtmp[:],
                                        retsb[cb][:, h * 128:(h + 1) * 128],
                                        AF.Square,
                                        accum_out=stats[:, 240 + h:241 + h])
                return last
            ssq = sc.step("scalar", fsq, waits=[er1, er2] +
                          ([pstat] if pstat else []), inc="Sc")
            vst4 = sc.step("vector", lambda v, b=b: v.tensor_tensor(
                statacc[:, 32 + b * 16:32 + (b + 1) * 16],
                statacc[:, 32 + b * 16:32 + (b + 1) * 16],
                stats[:, 240:256], ALU.add), waits=[ssq, vst2], inc="V")
            t3["prev_stats"] = vst4
            yield
            def fds(t, cb=cb):
                last = None
                for h in range(H):
                    last = t.matmul(ps_h[:, h * 128:(h + 1) * 128],
                                    khtok[cb][:, h * 128:(h + 1) * 128],
                                    vtok[cb][:, h * 128:(h + 1) * 128],
                                    start=(h % 4 == 0), stop=(h % 4 == 3))
                return last
            dsm = sc.step("tensor", fds, waits=[ekt, evt, er1, er2], inc="T")
            t3["sppr"][cb] = dsm
            wd = [stz[b]] + list(t3["prev_wst"][b])
            if t3["prev_cast"][b] is not None:
                wd.append(t3["prev_cast"][b])
            wst1 = sc.step("vector", lambda v, b=b: v.tensor_tensor(
                st_f[b][:], st_f[b][:], gcbc[:], ALU.mult), waits=wd, inc="V")
            wst2 = sc.step("vector", lambda v, b=b: v.tensor_tensor(
                st_f[b][:, 0:1024], st_f[b][:, 0:1024], ps_h[:, 0:1024], ALU.add),
                waits=[dsm, wst1], inc="V")
            wst3 = sc.step("vector", lambda v, b=b: v.tensor_tensor(
                st_f[b][:, 1024:2048], st_f[b][:, 1024:2048], ps_h[:, 1024:2048],
                ALU.add), waits=[dsm, wst1], inc="V")
            t3["prev_wst"][b] = (wst2, wst3)
            t3["psh_free"] = [wst2, wst3]
            cast = sc.step("scalar", lambda s, b=b: s.copy(st_b[b][:], st_f[b][:]),
                           waits=[wst2, wst3], inc="Sc")
            t3["prev_cast"][b] = cast
            t3["ld_gate"][cb] = (rmm3, gqh, gkh, gsm, dsm)

    if "P2" in stages:
        aorder = [0, 2, 1, 3]
        p3_ready = {}
        for i, tg in enumerate(aorder):
            b_ = tg // 2
            for c_ in range((tg % 2) * 8, (tg % 2) * 8 + 8):
                p3_ready[(c_, b_)] = i + 1
        p3_list = [(c_, b_) for c_ in range(NCH) for b_ in range(NB)]
        p3_state = {"idx": 0, "gen": None}

        def pump_p3(done_sub, credits):
            if "P3" not in stages:
                return
            n = 0
            while n < credits:
                if p3_state["gen"] is None:
                    if (p3_state["idx"] >= len(p3_list) or
                            p3_ready[p3_list[p3_state["idx"]]] > done_sub):
                        return
                    c_, b_ = p3_list[p3_state["idx"]]
                    p3_state["gen"] = p3_unit_gen(c_, b_)
                try:
                    next(p3_state["gen"])
                except StopIteration:
                    p3_state["gen"] = None
                    p3_state["idx"] += 1
                n += 1

        for i, tg in enumerate(aorder):
            emit_p2_tloads(tg)
            pump_p3(i, 4)
            for nm in ["q", "k", "v"]:
                for m in range(DT):
                    emit_p2_m(nm, tg, m)
                    pump_p3(i, 1)
        for i, tg in enumerate(aorder):
            emit_p2_tloads(tg)
            pump_p3(4, 4)
            for m in range(DT):
                emit_p2_m("g", tg, m)
                pump_p3(4, 2)
        pump_p3(4, 400)

    if "P3" in stages:
        gn_done = []
        for b in range(NB):
            def fgn(t, b=b):
                t.matmul(ps_r45[0:1, 0:16], onescol_f_sb[:],
                         statacc[:, b * 16:(b + 1) * 16], start=True, stop=False)
                return t.matmul(ps_r45[0:1, 16:32], onescol_f_sb[:],
                                statacc[:, 32 + b * 16:32 + (b + 1) * 16],
                                start=False, stop=True)
            gmm = sc.step("tensor", fgn,
                          waits=[t3["prev_stats"], prev_ev[0], prev_ev[1],
                                 t3["psh_free"][0], t3["psh_free"][1]] + gn_done,
                          inc="T")
            NSD = float(S * 128)
            g1 = sc.step("vector", lambda v: v.tensor_scalar(
                stats[0:1, 96:112], ps_r45[0:1, 0:16], 1.0 / NSD, None, ALU.mult),
                waits=[gmm], inc="V")
            g2 = sc.step("vector", lambda v: v.tensor_scalar(
                stats[0:1, 112:128], ps_r45[0:1, 16:32], 1.0 / NSD, None,
                ALU.mult), waits=[g1], inc="V")
            g3 = sc.step("vector", lambda v: v.tensor_tensor(
                stats[0:1, 240:256], stats[0:1, 96:112], stats[0:1, 96:112],
                ALU.mult), waits=[g2], inc="V")
            g4 = sc.step("vector", lambda v: v.tensor_tensor(
                stats[0:1, 112:128], stats[0:1, 112:128], stats[0:1, 240:256],
                ALU.subtract), waits=[g3], inc="V")
            g5 = sc.step("scalar", lambda s: s.activation(
                stats[0:1, 240:256], stats[0:1, 112:128], AF.Ln, bias=EPS),
                waits=[g4], inc="Sc")
            g6 = sc.step("scalar", lambda s: s.activation(
                stats[0:1, 112:128], stats[0:1, 240:256], AF.Exp, scale=-0.5),
                waits=[g5], inc="Sc")
            def fbc(t, b=b):
                t.matmul(ps_r45[:, 512:528], onesrow_f_sb[:],
                         stats[0:1, 112:128], start=True, stop=False)
                return t.matmul(ps_r45[:, 528:544], onesrow_f_sb[:],
                                stats[0:1, 96:112], start=False, stop=True)
            bmm = sc.step("tensor", fbc, waits=[g6], inc="T")
            g8 = sc.step("vector", lambda v, b=b: v.tensor_tensor(
                stats[:, 192 + b * 16:192 + (b + 1) * 16], biases[:, 3 * DT:4 * DT],
                ps_r45[:, 512:528], ALU.mult), waits=[bmm], inc="V")
            g9 = sc.step("vector", lambda v, b=b: v.tensor_tensor(
                stats[:, 240:256], stats[:, 192 + b * 16:192 + (b + 1) * 16],
                ps_r45[:, 528:544], ALU.mult), waits=[g8], inc="V")
            g10 = sc.step("vector", lambda v, b=b: v.tensor_tensor(
                stats[:, 224 + b * 16:224 + (b + 1) * 16], biases[:, 4 * DT:5 * DT],
                stats[:, 240:256], ALU.subtract), waits=[g9], inc="V")
            gn_done.append(g10)

    x3_stores = [None] * NTT
    x4t_stores = [None] * NTT
    if "P4" in stages:
        wo_home = ([xTg[:, kt * D:(kt + 1) * D] for kt in range(8)] +
                   [wtile[0][:], wtile[1][:], WSTG[1][:, 0:D], WSTG[1][:, D:2 * D],
                    None,
                    WSTG[0][:, D:2 * D], mexp[0][:], mexp[1][:]])

        def wo_ap(kt, oq):
            if kt == 12:
                return qkstage[oq // 2][:, (oq % 2) * 512:(oq % 2) * 512 + 512]
            return wo_home[kt][:, oq * 512:(oq + 1) * 512]

        wo_gate = ([sc.at("T"), sc.at("Sc"), sc.at("V"), wprep["o"][0],
                    wprep["o"][1], t3["prev_sro"][0], t3["prev_sro"][1]] +
                   list(proj_final.get(("g", 3), ())))
        wo_lds = []
        for kt in range(DT):
            if kt == 12:
                wo_lds.append(sc.dma("sync", "wo5", qkstage[0][:],
                                     woB[kt * 128:(kt + 1) * 128, 0:1024],
                                     waits=wo_gate))
                wo_lds.append(sc.dma("sync", "wo5", qkstage[1][:],
                                     woB[kt * 128:(kt + 1) * 128, 1024:2048],
                                     waits=wo_gate))
            else:
                wo_lds.append(sc.dma("sync", "wo5", wo_home[kt],
                                     woB[kt * 128:(kt + 1) * 128, :],
                                     waits=wo_gate))
        roch = [xbufB[0][:, 0:D], xbufB[0][:, D:2 * D]]
        gch2 = [xbufB[1][:, 0:D], xbufB[1][:, D:2 * D]]
        rogate = [hbuf[0][:], hbuf[1][:]]
        x1res = [xres[0], xres[1]]
        outb = [oub[0], oub[1]]
        p4v0 = sc.at("V")
        p4t0 = sc.at("T")
        p4s0 = sc.at("Sc")
        p4_psum = [(ps_r45[:], ps_sel[:]), (ps_h[:, 0:1024], ps_h[:, 1024:2048])]
        prev_ld4 = [None, None]
        prev_x1 = [None, None]
        prev_ps_ev = [[p4v0, p4v0], [p4v0, p4v0]]
        prev_rg = [None, None]
        prev_oub_st = [None, None]
        prev_ln_ap = [None, None]
        prev_ln_st = [None, None]
        p4_iters = [(b, c) for b in range(NB) for c in range(NCH)]
        preps = {}
        lx1s = {}

        def p4_prep(i):
            b, c = p4_iters[i]
            cb = i % 2
            wro = [t3["prev_sro"][0], t3["prev_sro"][1], gn_done[b]]
            if c == 0:
                wro.append(sc.at("G"))
            if prev_ld4[cb] is not None:
                wro.append(prev_ld4[cb])
            r3 = lambda ap: ap.rearrange("dt p t -> p dt t")
            s3 = lambda ap: ap.rearrange("p (dt t) -> p dt t", t=C)
            lro = sc.dma("gpsimd", f"ro4{cb}", s3(roch[cb]), r3(ROT[b, c]),
                         waits=wro)
            lg = sc.dma("gpsimd", f"g4{cb}", s3(gch2[cb]), r3(GT[b, c]),
                        waits=list(proj_final.get(("g", b * 2 + c // 8), ())) +
                              ([prev_ld4[cb]] if prev_ld4[cb] else []))
            lx1s[i] = sc.dma("sync", f"rx4{cb}", x1res[cb][:],
                             x1f[i * 128:(i + 1) * 128, :],
                             waits=[prev_x1[cb]] if prev_x1[cb] else [])
            tga = None
            for dt_ in range(DT):
                wga = [lro, gn_done[b]]
                if dt_ == 0 and prev_rg[cb] is not None:
                    wga.append(prev_rg[cb])
                tga = sc.step("vector", lambda v, cb=cb, b=b, dt_=dt_: v.tensor_scalar(
                    sjunk[:, dt_ * 128:(dt_ + 1) * 128],
                    roch[cb][:, dt_ * 128:(dt_ + 1) * 128],
                    stats[:, 192 + b * 16 + dt_:192 + b * 16 + dt_ + 1],
                    stats[:, 224 + b * 16 + dt_:224 + b * 16 + dt_ + 1],
                    ALU.mult, ALU.add), waits=wga, inc="V")
            vrg = sc.step("vector", lambda v, cb=cb: v.tensor_tensor(
                rogate[cb], sjunk[:], gch2[cb], ALU.mult), waits=[tga, lg],
                inc="V")
            prev_ld4[cb] = vrg
            preps[i] = vrg

        p4_prep(0)
        for b in range(NB):
            for c in range(NCH):
                i = b * NCH + c
                cb = i % 2
                if i + 1 < len(p4_iters):
                    p4_prep(i + 1)
                vrg = preps[i]
                lx1 = lx1s[i]
                pp = i % 2
                psA, psB2 = p4_psum[pp]
                def fop(t, cb=cb, pp=pp):
                    psA, psB2 = p4_psum[pp]
                    last = None
                    for oq in range(4):
                        tgt = psA if oq < 2 else psB2
                        col = (oq % 2) * 512
                        t.matmul(tgt[:, col:col + 512], onesrow_b_sb[:],
                                 biasrow[:, oq * 512:(oq + 1) * 512],
                                 start=True, stop=False)
                        for kt in range(DT):
                            last = t.matmul(
                                tgt[:, col:col + 512],
                                rogate[cb][:, kt * 128:(kt + 1) * 128],
                                wo_ap(kt, oq),
                                start=False, stop=(kt == DT - 1))
                    return last
                wop = [vrg, borow_tok] + prev_ps_ev[pp]
                if i == 0:
                    wop += [p4t0] + ([wo_lds[-1]] if wo_lds else [])
                opm = sc.step("tensor", fop, waits=wop, inc="T")
                prev_rg[cb] = opm
                ob4 = cb
                wev4 = [opm, lx1]
                if prev_oub_st[ob4] is not None:
                    wev4.append(prev_oub_st[ob4])
                if prev_ln_ap[ob4] is not None:
                    wev4.append(prev_ln_ap[ob4])
                e1 = sc.step("vector", lambda v, cb=cb, pp=pp: v.tensor_tensor(
                    outb[cb][:, 0:1024], p4_psum[pp][0], x1res[cb][:, 0:1024],
                    ALU.add), waits=wev4, inc="V")
                e2 = sc.step("vector", lambda v, cb=cb, pp=pp: v.tensor_tensor(
                    outb[cb][:, 1024:2048], p4_psum[pp][1], x1res[cb][:, 1024:2048],
                    ALU.add), waits=[e1], inc="V")
                prev_ps_ev[pp] = [e1, e2]
                prev_x1[cb] = e2
                st4 = sc.dma("scalar", f"so4{cb}", x3f[i * 128:(i + 1) * 128, :],
                             outb[cb][:], waits=[e2])
                x3_stores[i] = st4
                prev_oub_st[ob4] = st4
                c0 = 64 + (i % 4) * 8
                sq = sc.step("scalar", lambda s, cb=cb, c0=c0: s.activation(
                    WSTG[0][:, 0:D], outb[cb][:], AF.Square,
                    accum_out=stats[:, c0 + 1:c0 + 2]), waits=[e2, vrg], inc="Sc")
                rv = sc.step("vector", lambda v, cb=cb, c0=c0: v.reduce_sum(
                    stats[:, c0:c0 + 1], outb[cb][:], axis=AX), waits=[e2], inc="V")
                vmu = sc.step("vector", lambda v, c0=c0: v.tensor_scalar(
                    stats[:, c0 + 2:c0 + 3], stats[:, c0:c0 + 1], 1.0 / D, None,
                    ALU.mult), waits=[rv], inc="V")
                vm2_ = sc.step("vector", lambda v, c0=c0: v.tensor_tensor(
                    stats[:, c0 + 3:c0 + 4], stats[:, c0 + 2:c0 + 3],
                    stats[:, c0 + 2:c0 + 3], ALU.mult), waits=[vmu], inc="V")
                vva = sc.step("vector", lambda v, c0=c0: v.scalar_tensor_tensor(
                    stats[:, c0 + 4:c0 + 5], stats[:, c0 + 1:c0 + 2], 1.0 / D,
                    stats[:, c0 + 3:c0 + 4], ALU.mult, ALU.subtract),
                    waits=[vm2_, sq], inc="V")
                sl = sc.step("scalar", lambda s, c0=c0: s.activation(
                    stats[:, c0 + 5:c0 + 6], stats[:, c0 + 4:c0 + 5], AF.Ln,
                    bias=EPS), waits=[vva], inc="Sc")
                se = sc.step("scalar", lambda s, c0=c0: s.activation(
                    stats[:, c0 + 6:c0 + 7], stats[:, c0 + 5:c0 + 6], AF.Exp,
                    scale=-0.5), waits=[sl], inc="Sc")
                lw = [se]
                if prev_ln_st[cb] is not None:
                    lw.append(prev_ln_st[cb])
                else:
                    lw += [sc.at("sx0"), sc.at("sx1")]
                vap = sc.step("vector", lambda v, cb=cb, c0=c0: v.tensor_scalar(
                    xtout[cb][:], outb[cb][:], stats[:, c0 + 2:c0 + 3],
                    stats[:, c0 + 6:c0 + 7], ALU.subtract, ALU.mult),
                    waits=lw, inc="V")
                prev_ln_ap[cb] = vap
                st5 = sc.dma("scalar", f"sx4{cb}", x4t[i * 128:(i + 1) * 128, :],
                             xtout[cb][:], waits=[vap])
                x4t_stores[i] = st5
                prev_ln_st[cb] = st5
                ld_sq_guard = sq

    if "P5" in stages:
        p5_entry = [brrow_tok, prep_tok, sc.at("T"), sc.at("V"), sc.at("Sc"),
                    sc.at("G"), x3_stores[NTT - 1], x3_stores[NTT - 2]]
        ffn_g5, y_stores, _ = emit_ffn_pass(x4t, x3f, y, x4t_stores,
                                            res_tile_waits=x3_stores, ln_dst=None,
                                            scol0=64, entry_waits=p5_entry)
        for g in range(NG):
            ffn_g5(g)
        fin = [sc.at("so0"), sc.at("so1")]
    else:
        fin = []

    drain = [sc.at(s) for s in list(sc.cnt) if s not in ("T", "V", "Sc", "G")]
    sc.step("sync", lambda e: e.nop(), waits=drain + fin)

    sc.emit_and_close()
    return nc


def _consts():
    t = np.arange(128, dtype=np.float32)
    expnT_ = t[None, :] - t[:, None]
    expnT_[expnT_ < 0] = 1.0e6
    tpow_ = np.tile(t + 1.0, H)[None, :]
    HREP_ = np.zeros((H, D), dtype=np.float32)
    for h in range(H):
        HREP_[h, h * 128:(h + 1) * 128] = 1.0
    Esel_ = np.zeros((NE, RA), dtype=np.float32)
    for e in range(NE):
        Esel_[e, e * RANK:(e + 1) * RANK] = 1.0
    import ml_dtypes
    bf = lambda a: a.astype(ml_dtypes.bfloat16)
    return dict(
        identf=np.eye(128, dtype=np.float32),
        identb=bf(np.eye(128, dtype=np.float32)),
        onesrow_b=bf(np.ones((1, 128), np.float32)),
        onesrow_f=np.ones((1, 128), np.float32),
        ones128f=np.ones((128, 128), np.float32),
        onescol_f=np.ones((128, 1), np.float32),
        expnT=expnT_.astype(np.float32),
        tpow=tpow_.astype(np.float32),
        HREP=HREP_, Esel=bf(Esel_),
        cvals=np.tile(np.array([[1e-5, -0.5, 128.0, 0.5 / 64, 1.0 / 2048, 2.0, 0, 0]],
                                dtype=np.float32), (128, 1)),
    )


def make_inmaps(inputs):
    consts = _consts()
    x = np.ascontiguousarray(np.asarray(inputs["L0_attn_x"], dtype=np.float32))
    shards = np.split(x.reshape(16 * S, D), 8, axis=0)
    base = dict(
        wq=np.ascontiguousarray(inputs["L0_attn_wq"], dtype=np.float32),
        wk=np.ascontiguousarray(inputs["L0_attn_wk"], dtype=np.float32),
        wv=np.ascontiguousarray(inputs["L0_attn_wv"], dtype=np.float32),
        wg=np.ascontiguousarray(inputs["L0_attn_wg"], dtype=np.float32),
        wo=np.ascontiguousarray(inputs["L0_attn_wo"], dtype=np.float32),
        bq=np.ascontiguousarray(inputs["L0_attn_bq"], dtype=np.float32),
        bk=np.ascontiguousarray(inputs["L0_attn_bk"], dtype=np.float32),
        bv=np.ascontiguousarray(inputs["L0_attn_bv"], dtype=np.float32),
        bg=np.ascontiguousarray(inputs["L0_attn_bg"], dtype=np.float32),
        bo=np.ascontiguousarray(inputs["L0_attn_bo"], dtype=np.float32),
        gnw=np.ascontiguousarray(inputs["L0_attn_gn_w"], dtype=np.float32),
        gnb=np.ascontiguousarray(inputs["L0_attn_gn_b"], dtype=np.float32),
        gamma=np.ascontiguousarray(inputs["L0_attn_gamma"], dtype=np.float32),
        fAT=np.ascontiguousarray(
            np.asarray(inputs["L0_ffn_A"], dtype=np.float32)
            .reshape(NE, RANK, DT, 128).transpose(3, 2, 0, 1).reshape(128, DT * RA)),
        fBT=np.ascontiguousarray(
            np.asarray(inputs["L0_ffn_B"], dtype=np.float32)
            .reshape(4, 2, D, RANK).transpose(1, 3, 0, 2).reshape(128, 4 * D)),
        fWrT=np.ascontiguousarray(
            np.asarray(inputs["L0_ffn_Wr"], dtype=np.float32)
            .reshape(NE, DT, 128).transpose(2, 1, 0).reshape(128, DT * NE)),
        fbr=np.ascontiguousarray(np.asarray(inputs["L0_ffn_br"],
                                            dtype=np.float32).reshape(1, NE)),
        **consts,
    )
    return [dict(base, xs=np.ascontiguousarray(s)) for s in shards]


def kernel(**inputs):
    nc = build(stages=("P1", "P2", "P3", "P4", "P5"))
    in_maps = make_inmaps(inputs)
    res = run_bass_kernel_spmd(nc, in_maps, core_ids=list(range(8)))
    out = np.concatenate([r["y"] for r in res.results], axis=0)
    return out.reshape(16, S, D)
